# version 08
# baseline (speedup 1.0000x reference)
import numpy as np
import ml_dtypes
from contextlib import ExitStack

import concourse.bass as bass
import concourse.mybir as mybir
from concourse.bass_utils import run_bass_kernel_spmd

F32 = mybir.dt.float32
BF16 = mybir.dt.bfloat16
FP8 = mybir.dt.float8e4
AF = mybir.ActivationFunctionType
FADD = True
PV_DEFER = 2
P3N = 3
ALU = mybir.AluOpType
AX = mybir.AxisListType
bfd = ml_dtypes.bfloat16
f8d = ml_dtypes.float8_e4m3fn

T = 2048
DM = 1024
H = 8
D = 128
BS = 16
WIN = 128
SCALE = 0.12
NT = 16
NG = 4
NEG = -30000.0
BIGSUP = -1e30
SELTH = -1e29


class Build:
    ENGINES = ("pe", "act", "dve", "pool", "sp")

    def __init__(self):
        self.q = {e: [] for e in self.ENGINES}
        self.cnt = {e: 0 for e in self.ENGINES}
        self.dma_cnt = {}
        self.sem_names = set(self.ENGINES)

    def op(self, eng, fn):
        self.q[eng].append(("op", fn, None))
        return None

    def op_inc(self, eng, fn):
        self.cnt[eng] += 1
        self.q[eng].append(("op", fn, eng))
        return (eng, self.cnt[eng])

    def dma(self, queue, fn, sem):
        self.sem_names.add(sem)
        self.dma_cnt[sem] = self.dma_cnt.get(sem, 0) + 16
        self.q[queue].append(("op", fn, sem))
        return (sem, self.dma_cnt[sem])

    def wait(self, eng, token):
        if token is None:
            return
        sem_name, val = token
        if val <= 0:
            return
        self.q[eng].append(("wait", sem_name, val))

    def replay(self, eng_name, eng_obj, sems):
        for kind, a, c in self.q[eng_name]:
            if kind == "wait":
                eng_obj.wait_ge(sems[a], c)
            else:
                inst = a(eng_obj)
                if c is not None:
                    inst.then_inc(sems[c], 16 if c not in self.ENGINES else 1)


def _rotary_tables():
    n = D // 4
    ang = (1.0 / 1024.0) ** np.linspace(0.0, 1.0, n)
    ang = np.concatenate([ang, np.zeros(D // 2 - n)])
    th = np.arange(T, dtype=np.float64)[:, None] * ang[None, :]
    cos8 = np.tile(np.cos(th), (1, H)).astype(np.float32).astype(bfd)
    sin8 = np.tile(np.sin(th), (1, H)).astype(np.float32).astype(bfd)
    return cos8, sin8


def _static_masks():
    kj = np.arange(128)[:, None]
    v = np.arange(1024)[None, :] - 384
    msm = ((v - kj >= 0) & (v - kj <= WIN - 1)).astype(np.float32).astype(bfd)
    cbm = ((v - kj >= 0) & ((v // BS) == (kj // BS))).astype(np.float32).astype(bfd)
    cmm = (v - kj >= 0).astype(np.float32).astype(bfd)
    u = np.arange(256)[None, :]
    tl = np.arange(128)[:, None]
    cmaster = np.where(16 * (u - 120) <= tl, 0.0, NEG).astype(np.float32).astype(bfd)
    ef = np.zeros((NT, 128, 128), np.float32)
    for kt in range(NT):
        k = np.arange(128)
        ef[kt, 8 * kt + k // BS, k] = 1.0
    return msm, cbm, cmm, cmaster, ef.astype(f8d)


def host_prepare(inp):
    f32 = np.float32
    g = lambda k: np.ascontiguousarray(np.asarray(inp[k], f32))
    x = g("L0_attn_x")
    qkv = g("L0_attn_qkv_w")
    lam = g("L0_attn_lambdas")
    cpj = g("L0_attn_c_proj_w")
    kfc = g("L0_attn_k_fc_w")
    kpj = g("L0_attn_k_pj_w")
    vfc = g("L0_attn_v_fc_w")
    vpj = g("L0_attn_v_pj_w")
    mem = g("L0_attn_mem_kv")
    kpos = g("L0_attn_k_pos")
    vpos = g("L0_attn_v_pos")
    sw = g("L0_attn_strat_w")
    sb = g("L0_attn_strat_b")
    wr = g("L0_ffn_W_r")
    wk = g("L0_ffn_W_k")
    wv = g("L0_ffn_W_v")
    mur = g("L0_ffn_mu_r")
    muk = g("L0_ffn_mu_k")

    C = np.ascontiguousarray
    shared = {
        "QKWQ": C(qkv[0].T.reshape(8, 128, 1024)),
        "QKWK": C(qkv[1].T.reshape(8, 128, 1024)),
        "QWV": C(qkv[2].T.reshape(8, 128, 8, 128).transpose(2, 1, 0, 3).reshape(8, 128, 1024)),
        "STRATW": C(sw.T.reshape(8, 128, 24)),
        "B24": C(np.tile(sb[None, :], (128, 1))),
        "LAM0": np.broadcast_to(lam[0:1], (128, 1)).copy(),
        "FCK": C(kfc.reshape(64, 128, 16, 128).transpose(0, 3, 2, 1).reshape(64, 128, 2048)),
        "FCV": C(vfc.reshape(64, 128, 16, 128).transpose(0, 3, 2, 1).reshape(64, 128, 2048)),
        "PJK": C(kpj.T.reshape(64, 128, 128).transpose(1, 0, 2).reshape(128, 8192)),
        "PJV": C(vpj.T.reshape(64, 128, 128).transpose(1, 0, 2).reshape(128, 8192)),
        "MEMKT": C(mem[0].reshape(8, 128).T),
        "MEMV": C(mem[1].reshape(1, 1024)),
        "KPOST": C(kpos.transpose(2, 0, 1).reshape(128, 128)),
        "VPOST": C(vpos.transpose(2, 0, 1).reshape(128, 128)),
        "CPW": C(cpj.T.reshape(8, 128, 1024)),
        "WRT": C(wr.T.reshape(8, 128, 1024)),
        "WKT": C(wk.T.reshape(8, 128, 1024)),
        "WVT": C(wv.T.reshape(8, 128, 1024)),
        "MUR": C(mur.reshape(8, 128).T),
        "MUK": C(muk.reshape(8, 128).T),
        "IDB": np.eye(128, dtype=np.float32).astype(bfd),
        "ONESB": np.ones((128, 1), np.float32).astype(bfd),
        "ONESR": np.ones((1, 128), np.float32).astype(bfd),
    }
    cos8, sin8 = _rotary_tables()
    shared["COS8"] = C(cos8.reshape(NT, 128, 512))
    shared["SIN8"] = C(sin8.reshape(NT, 128, 512))
    msm, cbm, cmm, cmaster, ef = _static_masks()
    shared["MSM"] = msm
    shared["CBM"] = cbm
    shared["CM"] = cmm
    shared["CMASTER"] = cmaster
    shared["EFULL"] = C(ef.transpose(1, 0, 2).reshape(128, NT * 128))
    in_maps = []
    for c in range(8):
        m = dict(shared)
        m["X"] = C(x[c])
        in_maps.append(m)
    return in_maps


def build_program(dbg=(), phases=99):
    nc = bass.Bass("TRN2", target_bir_lowering=False, debug=False)
    dt = nc.dram_tensor
    X = dt("X", [T, DM], F32, kind="ExternalInput").ap()
    QKWQ = dt("QKWQ", [8, 128, 1024], F32, kind="ExternalInput").ap()
    QKWK = dt("QKWK", [8, 128, 1024], F32, kind="ExternalInput").ap()
    QWV = dt("QWV", [8, 128, 1024], F32, kind="ExternalInput").ap()
    STRATW = dt("STRATW", [8, 128, 24], F32, kind="ExternalInput").ap()
    B24 = dt("B24", [128, 24], F32, kind="ExternalInput").ap()
    LAM0 = dt("LAM0", [128, 1], F32, kind="ExternalInput").ap()
    FCK = dt("FCK", [64, 128, 2048], F32, kind="ExternalInput").ap()
    FCV = dt("FCV", [64, 128, 2048], F32, kind="ExternalInput").ap()
    PJK = dt("PJK", [128, 8192], F32, kind="ExternalInput").ap()
    PJV = dt("PJV", [128, 8192], F32, kind="ExternalInput").ap()
    MEMKT = dt("MEMKT", [128, 8], F32, kind="ExternalInput").ap()
    MEMV = dt("MEMV", [1, 1024], F32, kind="ExternalInput").ap()
    KPOST = dt("KPOST", [128, 128], F32, kind="ExternalInput").ap()
    VPOST = dt("VPOST", [128, 128], F32, kind="ExternalInput").ap()
    CPW = dt("CPW", [8, 128, 1024], F32, kind="ExternalInput").ap()
    WRT = dt("WRT", [8, 128, 1024], F32, kind="ExternalInput").ap()
    WKT = dt("WKT", [8, 128, 1024], F32, kind="ExternalInput").ap()
    WVT = dt("WVT", [8, 128, 1024], F32, kind="ExternalInput").ap()
    MUR = dt("MUR", [128, 8], F32, kind="ExternalInput").ap()
    MUK = dt("MUK", [128, 8], F32, kind="ExternalInput").ap()
    IDB = dt("IDB", [128, 128], BF16, kind="ExternalInput").ap()
    ONESBd = dt("ONESB", [128, 1], BF16, kind="ExternalInput").ap()
    ONESRd = dt("ONESR", [1, 128], BF16, kind="ExternalInput").ap()
    COS8 = dt("COS8", [NT, 128, 512], BF16, kind="ExternalInput").ap()
    SIN8 = dt("SIN8", [NT, 128, 512], BF16, kind="ExternalInput").ap()
    MSMd = dt("MSM", [128, 1024], BF16, kind="ExternalInput").ap()
    CBMd = dt("CBM", [128, 1024], BF16, kind="ExternalInput").ap()
    CMd = dt("CM", [128, 1024], BF16, kind="ExternalInput").ap()
    CMASTERd = dt("CMASTER", [128, 256], BF16, kind="ExternalInput").ap()
    EFULLd = dt("EFULL", [128, NT * 128], FP8, kind="ExternalInput").ap()

    XOUT = dt("XOUT", [T, DM], F32, kind="ExternalOutput").ap()
    X1SPILL = dt("X1SPILL", [T, DM], F32, kind="Internal").ap()
    dbg_t = {}
    for name, shape, dty in [
        ("QFM", [128, 16384], BF16), ("KFM", [128, 16384], BF16),
        ("VFM", [128, 16384], BF16), ("VTOK", [128, 16384], BF16),
        ("STRAT", [128, NT * 24], F32), ("CKFM", [128, 1024], BF16),
        ("CVTOK", [128, 1024], BF16), ("CVMEM", [1, 1024], BF16),
        ("KCALL", [128, 16384], BF16), ("SELT", [NG, 128, 512], BF16),
        ("ATT", [NG, 128, 4096], BF16), ("X1", [T, DM], F32),
        ("XT0", [128, 1024], F32), ("LNTOK0", [128, 1024], BF16),
        ("YFM", [128, 16392], BF16), ("KAT", [128, 16384], BF16),
        ("RSIG15", [128, 1024], BF16), ("X1LD15", [128, 1024], F32),
        ("KCALLK", [128, 16384], BF16),
        ("LNT", [128, 16384], BF16), ("QTOK0", [128, 1024], BF16),
        ("ROT0", [128, 2048], BF16),
    ]:
        if name in dbg:
            dbg_t[name] = dt("DBG_" + name, shape, dty, kind="ExternalOutput").ap()

    b = Build()
    st = ExitStack()
    en = st.enter_context
    with st:
        R_qy = en(nc.sbuf_tensor("R_qy", [128, 16400], BF16))
        R_kka = en(nc.sbuf_tensor("R_kka", [128, 16384], BF16))
        R_w = en(nc.sbuf_tensor("R_w", [128, 16384], BF16))
        R_vs = en(nc.sbuf_tensor("R_vs", [128, 16384], BF16))
        R_kc = en(nc.sbuf_tensor("R_kc", [128, 16384], BF16))
        R_ps = en(nc.sbuf_tensor("R_ps", [128, 9728], BF16))
        R_ws = en(nc.sbuf_tensor("R_ws", [128, 6144], BF16))

        idb = en(nc.sbuf_tensor("idb", [128, 128], BF16))
        onesb = en(nc.sbuf_tensor("onesb", [128, 1], BF16))
        onesr = en(nc.sbuf_tensor("onesr", [1, 128], BF16))
        cmm = en(nc.sbuf_tensor("cmm", [128, 1024], BF16))
        cmaster = en(nc.sbuf_tensor("cmaster", [128, 256], BF16))
        efull = en(nc.sbuf_tensor("efull", [128, NT * 128], FP8))
        memkt = en(nc.sbuf_tensor("memkt", [128, 8], F32))
        memkt_bf = en(nc.sbuf_tensor("memkt_bf", [128, 8], BF16))
        lam0 = en(nc.sbuf_tensor("lam0", [128, 1], F32))
        b24x = en(nc.sbuf_tensor("b24x", [128, 24], F32))
        b24b = en(nc.sbuf_tensor("b24b", [1, 48], BF16))
        mur_sb = en(nc.sbuf_tensor("mur_sb", [128, 8], F32))
        muk_sb = en(nc.sbuf_tensor("muk_sb", [128, 8], F32))
        omur_sb = en(nc.sbuf_tensor("omur_sb", [128, 8], F32))
        omuk_sb = en(nc.sbuf_tensor("omuk_sb", [128, 8], F32))
        stratw_sb = en(nc.sbuf_tensor("stratw_sb", [128, 8 * 24], BF16))
        strat_sig = en(nc.sbuf_tensor("strat_sig", [128, NT * 24], F32))
        ck_fm = en(nc.sbuf_tensor("ck_fm", [128, 1024], BF16))
        cv_tok = en(nc.sbuf_tensor("cv_tok", [128, 1024], BF16))
        stat_scr = [en(nc.sbuf_tensor(f"stat{i}", [128, 8], F32)) for i in range(3)]
        rr_tab = [en(nc.sbuf_tensor(f"rr_tab{i}", [128, 16], F32)) for i in range(2)]
        pt_sb = [en(nc.sbuf_tensor(f"pt{i}", [128, 128], BF16)) for i in range(4)]
        csim_sb = [en(nc.sbuf_tensor(f"csim{i}", [128, 129], F32)) for i in range(4)]
        pn_sb = [en(nc.sbuf_tensor(f"pn{i}", [128, 129], BF16)) for i in range(4)]
        memsc = en(nc.sbuf_tensor("memsc", [128, 8], F32))
        small = [en(nc.sbuf_tensor(f"small{i}", [128, 8], F32)) for i in range(4)]
        qtab = en(nc.sbuf_tensor("qtab", [128, 32], F32))
        s24_scr = en(nc.sbuf_tensor("s24_scr", [128, 24], F32))
        epsT = en(nc.sbuf_tensor("epsT", [128, 4], F32))

        ps_q = en(nc.psum_tensor("ps_q", [128, 1024], F32))
        ps_k = en(nc.psum_tensor("ps_k", [128, 1024], F32))
        ps_v = en(nc.psum_tensor("ps_v", [128, 1024], F32))
        ps_misc = en(nc.psum_tensor("ps_misc", [128, 512], F32))
        ps_trb = en(nc.psum_tensor("ps_trb", [128, 768], BF16))
        ps_s24 = ps_misc[:, 0:24]
        ps_sums = ps_misc[:, 32:40]
        ps_tr = ps_trb[:, 0:512]
        ps_ptr = ps_trb[:, 512:768]

        q_fm = R_qy[:, 0:16384]
        y_fm = R_qy[:, 0:16392]
        k_fm = R_kka[:, 0:16384]
        kaT = R_kka[:, 0:16384]
        qkw_sb = R_w[:, 0:16384]
        cpw_sb = R_w[:, 0:8192]
        wrt_sb = R_w[:, 0:8192]
        wkt_sb = R_w[:, 8192:16384]
        v_fm = R_vs[:, 0:16384]
        v_tok = R_vs[:, 0:16384]
        x_tile = [R_vs[:, i * 2048:(i + 1) * 2048].bitcast(F32) for i in range(3)]
        lnT = R_kc[:, 0:16384]
        kcall = R_kc[:, 0:16384]
        m_f32 = R_kc[:, 0:4096].bitcast(F32)
        sel_bf = R_kc[:, 4096:4608]
        mf01 = R_kc[:, 4608:12800]
        selt = R_kc[:, 12800:13312]
        eq_scr = R_kc[:, 13312:14336].bitcast(F32)
        selt250 = R_kc[:, 13312:13824]
        e2c = R_kc[:, 0:1024]
        pf2c = R_kc[:, 1024:2048]
        pslc = R_kc[:, 2048:3072]
        mfd_scr = R_kc[:, 14336:14848]
        wvt6a = R_kc[:, 0:8192]
        qt_scr2 = [R_ps[:, 0:1024], R_ws[:, 0:1024]]
        kt_scr2 = [R_ps[:, 1024:2048], R_ws[:, 1024:2048]]
        ln_tok = [R_ps[:, 2048 + i * 1024: 3072 + i * 1024] for i in range(3)]
        cos_s = [R_ps[:, 5120 + i * 512: 5632 + i * 512] for i in range(2)]
        sin_s = [R_ps[:, 6144 + i * 512: 6656 + i * 512] for i in range(2)]
        rot_q = [R_ps[:, 7168:8192], R_ws[:, 2048:3072]]
        rot_k = [R_ps[:, 8192:9216], R_ws[:, 3072:4096]]
        rot_tmp = R_ps[:, 9216:9728]
        disc = R_vs[:, 6144:7168]
        sq_scr = R_ps[:, 8704:9728]
        hid_f32 = [R_ps[:, i * 2048:(i + 1) * 2048].bitcast(F32) for i in range(2)]
        hid_bf = [R_ps[:, 4096 + i * 1024: 5120 + i * 1024] for i in range(2)]
        cv_fm_scr = R_ps[:, 6144:7168]
        kpost = R_ps[:, 7168:7424].bitcast(F32)
        vpost = R_ps[:, 7424:7680].bitcast(F32)
        spill_ev = R_ps[:, 1024:3072].bitcast(F32)
        pv_ev = R_ps[:, 1024:2048]
        sums_ev = R_ps[:, 2048:2064].bitcast(F32)
        psl2 = [R_ps[:, 0:1024], R_ps[:, 8192:9216]]
        att_acc = R_ps[:, 3072:7168]
        y_tok2 = [R_ps[:, 0:1024], R_ps[:, 1024:2048]]
        qwv_s = [R_ws[:, i * 1024:(i + 1) * 1024] for i in range(2)]
        wfc_s = [R_ws[:, i * 2048:(i + 1) * 2048] for i in range(2)]
        pj_s = [R_ws[:, 4096 + i * 1024: 5120 + i * 1024] for i in range(2)]
        e2 = [R_ws[:, i * 1024:(i + 1) * 1024] for i in range(2)]
        pf2 = [R_ws[:, 2048 + i * 1024: 3072 + i * 1024] for i in range(2)]
        attT = R_ws[:, 0:1024]
        memv_bf = R_ps[0:1, 8192:9216]
        memvx = R_ps[:, 7168:8192]
        msm = R_ws[:, 4096:5120]
        cbm = R_ws[:, 5120:6144]
        wvt6b = R_ws[:, 0:6144]
        r_sig = R_vs[:, 0:1024]
        ka_tok = R_vs[:, 1024:2048]
        k2relu = R_vs[:, 2048:4096].bitcast(F32)
        xr_sc2 = [R_vs[:, 4096:5120], R_kc[:, 8192:9216]]
        xk_sc2 = [R_vs[:, 5120:6144], R_kc[:, 9216:10240]]
        x1_ld = [R_vs[:, 6144 + i * 2048: 8192 + i * 2048].bitcast(F32) for i in range(2)]
        x6_ld = [R_vs[:, 10240 + i * 2048: 12288 + i * 2048].bitcast(F32) for i in range(2)]
        mix_tmp = R_ps[:, 3072:4096]
        xout_sb = R_vs[:, 14336:16384].bitcast(F32)

        def ld(queue, dst, src, sem):
            return b.dma(queue, lambda e, d=dst, s=src: e.dma_start(out=d, in_=s), sem)

        t_const = None
        for dst, src in [(idb[:], IDB), (onesb[:], ONESBd), (onesr[:], ONESRd),
                         (cmm[:], CMd),
                         (cmaster[:], CMASTERd), (efull[:], EFULLd),
                         (memkt[:], MEMKT), (lam0[:], LAM0), (b24x[:], B24),
                         (mur_sb[:], MUR), (muk_sb[:], MUK)]:
            t_const = ld("sp", dst, src, "dconst")
        t_mkb = ld("pool", memkt_bf[:], MEMKT, "dconstp")
        t_stw = ld("pool", stratw_sb[:].rearrange("p (c n) -> p c n", n=24),
                   STRATW.rearrange("c p n -> p c n"), "dconstp")
        qk_view = qkw_sb.rearrange("p (c s co) -> p c s co", s=2, co=1024)
        ld("pool", qk_view[:, :, 0, :], QKWQ.rearrange("c p co -> p c co"), "dconstp")
        t_qkw = ld("pool", qk_view[:, :, 1, :], QKWK.rearrange("c p co -> p c co"), "dconstp")
        b.op("dve", lambda e: e.memset(epsT[:, 0:1], 1e-5))
        b.op("dve", lambda e: e.memset(epsT[:, 1:2], 1e-6))
        b.op("dve", lambda e: e.memset(epsT[:, 2:3], 10.48576))
        t_eps = b.op_inc("dve", lambda e: e.memset(epsT[:, 3:4], 6.931471805599453))
        b.wait("act", t_eps)
        b.wait("dve", t_const)
        b.op("dve", lambda e: e.tensor_scalar(out=omur_sb[:], in0=mur_sb[:], scalar1=1.0,
                                              scalar2=-1.0, op0=ALU.subtract, op1=ALU.mult))
        b.op("dve", lambda e: e.tensor_scalar(out=omuk_sb[:], in0=muk_sb[:], scalar1=1.0,
                                              scalar2=-1.0, op0=ALU.subtract, op1=ALU.mult))
        t_bh = b.op_inc("dve", lambda e: e.tensor_copy(out=b24b[:, 0:24], in_=b24x[0:1, :]))
        b.wait("dve", t_bh)
        t_bl = b.op_inc("dve", lambda e: e.tensor_sub(
            s24_scr[0:1, :], b24x[0:1, :], b24b[:, 0:24]))
        b.wait("dve", t_bl)
        t_b24 = b.op_inc("dve", lambda e: e.tensor_copy(
            out=b24b[:, 24:48], in_=s24_scr[0:1, :]))
        b.wait("pe", t_b24)

        t_x = [None] * (2 * NT)
        t_cs = [None] * NT
        t_x[0] = ld("sp", x_tile[0][:, 0:1024], X[0:128, :], "dx0")
        t_x[1] = ld("sp", x_tile[1][:, 0:1024], X[128:256, :], "dx1")
        t_x[2] = ld("sp", x_tile[2][:, 0:1024], X[256:384, :], "dx2")
        for i in range(2):
            ld("sp", cos_s[i], COS8[i], f"dcos{i}")
            t_cs[i] = ld("sp", sin_s[i], SIN8[i], f"dcos{i}")

        lnTe = [None] * NT
        t_ln_tok = [None] * NT
        ev_done = [None] * NT
        qk_mm_done = [None] * NT
        strat_ev = [None] * NT
        tr_ev_done = [None] * NT
        rot_done = [None] * NT
        OD = 1.0 / DM

        b.wait("pe", t_qkw)
        b.wait("pe", t_const)

        psvb = ps_v[:].bitcast(BF16)

        def emit_qk_transposes(i):
            b.wait("pe", rot_done[i])
            if i >= 1:
                b.wait("pe", tr_ev_done[i - 1])
            tqt = None
            for s, rbuf in ((0, rot_q[i % 2]), (1, rot_k[i % 2])):
                for h in range(H):
                    slot = s * 8 + h
                    out = ps_tr[:, slot * 128:(slot + 1) * 128] if slot < 4 else \
                        psvb[:, (slot - 4) * 128:(slot - 3) * 128]
                    fn = lambda e, rbuf=rbuf, h=h, out=out: e.transpose(
                        out=out, in_=rbuf[:, h * 128:(h + 1) * 128], identity=idb[:])
                    tqt = b.op_inc("pe", fn) if slot == 15 else b.op("pe", fn)
            b.wait("act", tqt)
            qv = q_fm.rearrange("p (h t) -> p h t", t=2048)
            kv = k_fm.rearrange("p (h t) -> p h t", t=2048)
            b.op("act", lambda e, i=i, qv=qv: e.copy(
                out=qv[:, 0:4, i * 128:(i + 1) * 128],
                in_=ps_tr[:].rearrange("p (c t) -> p c t", t=128)))
            b.op("act", lambda e, i=i, qv=qv: e.copy(
                out=qv[:, 4:8, i * 128:(i + 1) * 128],
                in_=psvb[:, 0:512].rearrange("p (c t) -> p c t", t=128)))
            tev = b.op_inc("act", lambda e, i=i, kv=kv: e.copy(
                out=kv[:, 0:8, i * 128:(i + 1) * 128],
                in_=psvb[:, 512:1536].rearrange("p (c t) -> p c t", t=128)))
            tr_ev_done[i] = tev

        RMSSC = 0.08838834764831845
        t_f2_tab = [None] * NT
        ln_head_u = [None] * NT

        def emit_ln_head(i):
            b3 = i % 3
            st = stat_scr[b3]
            b.wait("act", t_x[i])
            t_ssq = b.op_inc("act", lambda e, b3=b3, st=st: e.activation(
                out=disc, in_=x_tile[b3][:, 0:1024], func=AF.Square, scale=32.0,
                accum_out=st[:, 1:2]))
            b.wait("dve", t_x[i])
            t_sum = b.op_inc("dve", lambda e, b3=b3, st=st: e.reduce_sum(
                out=st[:, 0:1], in_=x_tile[b3][:, 0:1024], axis=AX.X))
            b.wait("dve", t_sum)
            b.wait("dve", t_ssq)
            b.op("dve", lambda e, st=st: e.tensor_scalar(
                out=st[:, 2:3], in0=st[:, 0:1], scalar1=OD, scalar2=None, op0=ALU.mult))
            ln_head_u[i] = b.op_inc("dve", lambda e, st=st: e.scalar_tensor_tensor(
                out=st[:, 4:5], in0=st[:, 0:1], scalar=st[:, 0:1], in1=st[:, 1:2],
                op0=ALU.mult, op1=ALU.subtract))

        def emit_ln_tail(i):
            b3 = i % 3
            st = stat_scr[b3]
            b.wait("act", ln_head_u[i])
            t_ln5 = b.op_inc("act", lambda e, st=st: e.activation(
                out=st[:, 5:6], in_=st[:, 4:5], func=AF.Ln, scale=-1.0,
                bias=epsT[:, 2:3]))
            b.wait("act", t_ln5)
            t_rstd = b.op_inc("act", lambda e, st=st: e.activation(
                out=st[:, 6:7], in_=st[:, 5:6], func=AF.Exp, scale=-0.5,
                bias=epsT[:, 3:4]))
            b.wait("dve", t_rstd)
            t_ln_tok[i] = b.op_inc("dve", lambda e, b3=b3, st=st: e.scalar_tensor_tensor(
                out=ln_tok[b3], in0=x_tile[b3][:, 0:1024], scalar=st[:, 2:3],
                in1=st[:, 6:7].broadcast_to([128, 1024]),
                op0=ALU.subtract, op1=ALU.mult))

        def emit_rotary(j):
            b2r = j % 2
            qscr, kscr = qt_scr2[b2r], kt_scr2[b2r]
            b.wait("dve", t_cs[j])
            b.wait("dve", t_f2_tab[j])
            b.wait("dve", ev_done[j])
            if j >= 2:
                b.wait("dve", tr_ev_done[j - 2])
            tile_rot = None
            for s, scr, rbuf in ((0, qscr, rot_q[b2r]), (1, kscr, rot_k[b2r])):
                v3 = lambda ap: ap.rearrange("p (h a j) -> p h a j", a=2, j=64)
                sv = v3(scr)
                rv = v3(rbuf)
                cs = cos_s[b2r].rearrange("p (h j) -> p h j", j=64)
                sn = sin_s[b2r].rearrange("p (h j) -> p h j", j=64)
                tm = rot_tmp.rearrange("p (h j) -> p h j", j=64)
                b.op("dve", lambda e, sv=sv, rv=rv, cs=cs: e.tensor_tensor(
                    out=rv[:, :, 0, :], in0=sv[:, :, 0, :], in1=cs, op=ALU.mult))
                b.op("dve", lambda e, sv=sv, rv=rv, sn=sn: e.tensor_tensor(
                    out=rv[:, :, 1, :], in0=sv[:, :, 0, :], in1=sn, op=ALU.mult))
                b.op("dve", lambda e, sv=sv, tm=tm, sn=sn: e.tensor_tensor(
                    out=tm, in0=sv[:, :, 1, :], in1=sn, op=ALU.mult))
                b.op("dve", lambda e, rv=rv, tm=tm: e.tensor_tensor(
                    out=rv[:, :, 0, :], in0=rv[:, :, 0, :], in1=tm, op=ALU.add))
                b.op("dve", lambda e, sv=sv, tm=tm, cs=cs: e.tensor_tensor(
                    out=tm, in0=sv[:, :, 1, :], in1=cs, op=ALU.mult))
                b.op("dve", lambda e, rv=rv, tm=tm: e.tensor_sub(
                    rv[:, :, 1, :], tm, rv[:, :, 1, :]))
                for h in range(H):
                    fn = lambda e, rbuf=rbuf, h=h, s=s, b2r=b2r: e.tensor_scalar(
                        out=rbuf[:, h * 128:(h + 1) * 128],
                        in0=rbuf[:, h * 128:(h + 1) * 128],
                        scalar1=rr_tab[b2r][:, s * 8 + h: s * 8 + h + 1], scalar2=None,
                        op0=ALU.mult)
                    if s == 1 and h == H - 1:
                        tile_rot = b.op_inc("dve", fn)
                    else:
                        b.op("dve", fn)
            rot_done[j] = tile_rot
            if j == 0 and "QTOK0" in dbg_t:
                b.wait("sp", ev_done[0])
                ld("sp", dbg_t["QTOK0"], qt_scr2[0], "ddbg")
            if j == 0 and "ROT0" in dbg_t:
                b.wait("sp", tile_rot)
                ld("sp", dbg_t["ROT0"][:, 0:1024], rot_q[0], "ddbg")
                ld("sp", dbg_t["ROT0"][:, 1024:2048], rot_k[0], "ddbg")

        emit_ln_head(0)
        emit_ln_tail(0)
        for i in range(NT):
            b3 = i % 3
            b2 = i % 2
            if i + 1 < NT:
                emit_ln_head(i + 1)
            if i == 0 and "XT0" in dbg_t:
                b.wait("sp", t_ln_tok[0])
                ld("sp", dbg_t["XT0"], x_tile[0][:, 0:1024], "ddbg")
            if i == 0 and "LNTOK0" in dbg_t:
                b.wait("sp", t_ln_tok[0])
                ld("sp", dbg_t["LNTOK0"], ln_tok[0], "ddbg")

            b.wait("pe", t_ln_tok[i])
            if i >= 3:
                b.wait("pe", tr_ev_done[i - 3])
            tgt1 = tgt2 = None
            for ci in range(8):
                out = ps_tr[:, ci * 128:(ci + 1) * 128] if ci < 4 else \
                    psvb[:, (ci - 4) * 128:(ci - 3) * 128]
                fn = lambda e, b3=b3, ci=ci, out=out: e.transpose(
                    out=out, in_=ln_tok[b3][:, ci * 128:(ci + 1) * 128], identity=idb[:])
                if ci == 3:
                    tgt1 = b.op_inc("pe", fn)
                elif ci == 7:
                    tgt2 = b.op_inc("pe", fn)
                else:
                    b.op("pe", fn)
            lv = lnT.rearrange("p (c t) -> p c t", t=2048)
            b.wait("dve", tgt1)
            eg1 = b.op_inc("dve", lambda e, i=i, lv=lv: e.tensor_copy(
                out=lv[:, 0:4, i * 128:(i + 1) * 128],
                in_=ps_tr[:].rearrange("p (c t) -> p c t", t=128)))
            b.wait("dve", tgt2)
            eg2 = b.op_inc("dve", lambda e, i=i, lv=lv: e.tensor_copy(
                out=lv[:, 4:8, i * 128:(i + 1) * 128],
                in_=psvb[:, 0:512].rearrange("p (c t) -> p c t", t=128)))
            lnTe[i] = eg2
            if i + 1 < NT:
                emit_ln_tail(i + 1)

            if i >= 1:
                b.wait("pe", ev_done[i - 1])
                b.wait("pe", strat_ev[i - 1])
            b.op("pe", lambda e: e.matmul(
                out=ps_s24, lhsT=onesr[:], rhs=b24b[:, 0:24],
                start=True, stop=False, skip_group_check=True))
            b.op("pe", lambda e: e.matmul(
                out=ps_s24, lhsT=onesr[:], rhs=b24b[:, 24:48],
                start=False, stop=False, skip_group_check=True))
            tqk = None
            for ci in range(8):
                if ci == 0:
                    b.wait("pe", eg1)
                if ci == 4:
                    b.wait("pe", eg2)
                lhs = lnT[:, ci * 2048 + i * 128: ci * 2048 + (i + 1) * 128]
                for hh in range(2):
                    b.op("pe", lambda e, lhs=lhs, ci=ci, hh=hh: e.matmul(
                        out=ps_q[:, hh * 512:(hh + 1) * 512], lhsT=lhs,
                        rhs=qkw_sb[:, ci * 2048 + hh * 512: ci * 2048 + (hh + 1) * 512],
                        start=(ci == 0), stop=(ci == 7), skip_group_check=True))
                    b.op("pe", lambda e, lhs=lhs, ci=ci, hh=hh: e.matmul(
                        out=ps_k[:, hh * 512:(hh + 1) * 512], lhsT=lhs,
                        rhs=qkw_sb[:, ci * 2048 + 1024 + hh * 512:
                                   ci * 2048 + 1024 + (hh + 1) * 512],
                        start=(ci == 0), stop=(ci == 7), skip_group_check=True))
                fn = lambda e, lhs=lhs, ci=ci: e.matmul(
                    out=ps_s24, lhsT=lhs, rhs=stratw_sb[:, ci * 24:(ci + 1) * 24],
                    start=False, stop=(ci == 7), skip_group_check=True)
                tqk = b.op_inc("pe", fn) if ci == 7 else b.op("pe", fn)
            qk_mm_done[i] = tqk

            b.wait("act", tqk)
            if i >= 2:
                b.wait("act", rot_done[i - 2])
            qscr, kscr = qt_scr2[b2], kt_scr2[b2]
            b.op("act", lambda e, qscr=qscr: e.copy(out=qscr, in_=ps_q[:]))
            tev2 = b.op_inc("act", lambda e, kscr=kscr: e.copy(out=kscr, in_=ps_k[:]))
            ev_done[i] = tev2
            strat_ev[i] = b.op_inc("act", lambda e, i=i: e.activation(
                out=strat_sig[:, i * 24:(i + 1) * 24], in_=ps_s24, func=AF.Sigmoid))
            trr = None
            for s, scr in ((0, qscr), (1, kscr)):
                for h in range(H):
                    fn = lambda e, scr=scr, s=s, h=h, b2=b2: e.activation(
                        out=disc[:, 0:128], in_=scr[:, h * 128:(h + 1) * 128],
                        func=AF.Square, scale=RMSSC,
                        accum_out=rr_tab[b2][:, s * 8 + h: s * 8 + h + 1])
                    trr = b.op_inc("act", fn) if (s, h) == (1, 7) else b.op("act", fn)

            if i >= 2:
                emit_qk_transposes(i - 2)

            b.wait("act", trr)
            t_fln = b.op_inc("act", lambda e, b2=b2: e.activation(
                out=rr_tab[b2][:], in_=rr_tab[b2][:], func=AF.Ln, bias=epsT[:, 1:2]))
            b.wait("act", t_fln)
            t_f2_tab[i] = b.op_inc("act", lambda e, b2=b2: e.activation(
                out=rr_tab[b2][:], in_=rr_tab[b2][:], func=AF.Exp, scale=-0.5))

            if i >= 1:
                emit_rotary(i - 1)
            if i + 3 < NT:
                b.wait("sp", t_ln_tok[i])
                t_x[i + 3] = ld("sp", x_tile[b3][:, 0:1024],
                                X[(i + 3) * 128:(i + 4) * 128, :], f"dx{b3}")
            if i >= 1 and i + 1 < NT:
                b2n = (i + 1) % 2
                b.wait("sp", rot_done[i - 1])
                ld("sp", cos_s[b2n], COS8[i + 1], f"dcos{b2n}")
                t_cs[i + 1] = ld("sp", sin_s[b2n], SIN8[i + 1], f"dcos{b2n}")

        emit_rotary(NT - 1)
        emit_qk_transposes(NT - 2)
        emit_qk_transposes(NT - 1)

        t_qwv = [None] * 8
        b.wait("pool", rot_done[NT - 1])
        for idx in range(2):
            t_qwv[idx] = ld("pool", qwv_s[idx], QWV[idx], f"dqwv{idx}")
        v_ev = [None] * 8
        b.wait("pe", qk_mm_done[NT - 1])
        b.wait("pe", tr_ev_done[NT - 1])
        b.wait("act", rot_done[NT - 1])
        for co in range(8):
            wb = co % 2
            pbank = co % 2
            b.wait("pe", t_qwv[co])
            if co >= 2:
                b.wait("pe", v_ev[co - 2])
            for tg in range(4):
                tvm = None
                for ci in range(8):
                    fn = lambda e, ci=ci, wb=wb, tg=tg, pbank=pbank: e.matmul(
                        out=ps_v[:, pbank * 512:(pbank + 1) * 512],
                        lhsT=qwv_s[wb][:, ci * 128:(ci + 1) * 128],
                        rhs=lnT[:, ci * 2048 + tg * 512: ci * 2048 + (tg + 1) * 512],
                        start=(ci == 0), stop=(ci == 7), skip_group_check=True)
                    tvm = b.op_inc("pe", fn) if ci == 7 else b.op("pe", fn)
                b.wait("act", tvm)
                tko = b.op_inc("act", lambda e, co=co, tg=tg, pbank=pbank: e.activation(
                    out=v_fm[:, co * 2048 + tg * 512: co * 2048 + (tg + 1) * 512],
                    in_=ps_v[:, pbank * 512:(pbank + 1) * 512], func=AF.Copy,
                    scale=lam0[:, 0:1]))
                if tg < 3:
                    b.wait("pe", tko)
                else:
                    v_ev[co] = tko
            if co + 2 < 8:
                b.wait("pool", v_ev[co])
                t_qwv[co + 2] = ld("pool", qwv_s[wb], QWV[co + 2], f"dqwv{wb}")
        t_v_done = v_ev[7]

        b.wait("pool", rot_done[NT - 1])
        b.wait("pool", tr_ev_done[NT - 1])
        ld("pool", kpost[:, 0:128], KPOST, "dpost")
        t_post = ld("pool", vpost[:, 0:128], VPOST, "dpost")

        if "LNT" in dbg_t:
            b.wait("sp", qk_mm_done[0])
            ld("sp", dbg_t["LNT"], lnT, "ddbg")
        if "QFM" in dbg_t:
            b.wait("sp", tr_ev_done[NT - 1])
            ld("sp", dbg_t["QFM"], q_fm, "ddbg")
            ld("sp", dbg_t["KFM"], k_fm, "ddbg")
        if "VFM" in dbg_t:
            b.wait("sp", t_v_done)
            ld("sp", dbg_t["VFM"], v_fm, "ddbg")
        if "STRAT" in dbg_t:
            b.wait("sp", strat_ev[NT - 1])
            ld("sp", dbg_t["STRAT"], strat_sig[:], "ddbg")
        if phases <= 2:
            _finish(b, nc, en)
            return nc, dbg_t

        b.wait("pool", qk_mm_done[NT - 1])
        t_cpw = ld("pool", cpw_sb.rearrange("p (c n) -> p c n", n=1024),
                    CPW.rearrange("c p n -> p c n"), "dcpw")
        t_wkt = ld("pool", wkt_sb.rearrange("p (c n) -> p c n", n=1024),
                    WKT.rearrange("c p n -> p c n"), "dcpw")

        b.wait("dve", t_v_done)
        b.wait("dve", tr_ev_done[NT - 1])
        b.wait("dve", t_post)
        t_kc_asm = None
        for j in range(16):
            for h in range(H):
                fn = lambda e, j=j, h=h: e.tensor_scalar(
                    out=kcall[:, j * 1024 + h * 128: j * 1024 + (h + 1) * 128],
                    in0=k_fm[:, h * 2048 + j: h * 2048 + 2048: 16],
                    scalar1=kpost[:, h * 16 + j: h * 16 + j + 1], scalar2=None,
                    op0=ALU.add)
                t_kc_asm = b.op_inc("dve", fn) if (j, h) == (15, 7) else b.op("dve", fn)

        if "KCALLK" in dbg_t:
            b.wait("sp", t_kc_asm)
            t_kdump = ld("sp", dbg_t["KCALLK"], kcall, "ddbg")
            b.wait("dve", t_kdump)

        def fc_pass(tag, FCW, PJW, asm_token, out_evac, entry_war=(), entry_pool=()):
            t_w = [None] * 64
            t_pj = [None] * 8
            for tw in entry_pool:
                b.wait("pool", tw)
            for idx in range(2):
                t_w[idx] = ld("pool", wfc_s[idx], FCW[idx], f"dwf{tag}{idx}")
                t_pj[idx] = ld("pool", pj_s[idx], PJW[:, idx * 1024:(idx + 1) * 1024],
                               f"dpj{tag}{idx}")
            relu_done = [None, None]
            square_done = [None, None]
            pj_done = [None, None]
            b.wait("pe", asm_token)
            for tw in entry_war:
                b.wait("pe", tw)

            def emit_pj(c):
                buf = c % 2
                g8 = c // 8
                b.wait("pe", square_done[buf])
                if c % 8 == 0:
                    b.wait("pe", t_pj[g8])
                b.op("pe", lambda e, c=c, buf=buf, g8=g8: e.matmul(
                    out=ps_v[:, 0:512],
                    lhsT=pj_s[g8 % 2][:, (c % 8) * 128:(c % 8 + 1) * 128],
                    rhs=hid_bf[buf][:, 0:512],
                    start=(c == 0), stop=(c == 63), skip_group_check=True))
                tpj = b.op_inc("pe", lambda e, c=c, buf=buf, g8=g8: e.matmul(
                    out=ps_v[:, 512:1024],
                    lhsT=pj_s[g8 % 2][:, (c % 8) * 128:(c % 8 + 1) * 128],
                    rhs=hid_bf[buf][:, 512:1024],
                    start=(c == 0), stop=(c == 63), skip_group_check=True))
                pj_done[buf] = tpj
                if c % 8 == 7 and c + 8 < 64:
                    b.wait("pool", tpj)
                    t_pj[c // 8 + 1] = ld("pool", pj_s[(c // 8 + 1) % 2],
                                          PJW[:, (c // 8 + 1) * 1024:(c // 8 + 2) * 1024],
                                          f"dpj{tag}{(c // 8 + 1) % 2}")
                return tpj

            tpj = None
            for c in range(64):
                buf = c % 2
                hidps = ps_q if buf == 0 else ps_k
                b.wait("pe", t_w[c])
                b.wait("pe", relu_done[buf])
                tfc = None
                for j in range(16):
                    for hh in range(2):
                        fn = lambda e, j=j, hh=hh, buf=buf, hidps=hidps: e.matmul(
                            out=hidps[:, hh * 512:(hh + 1) * 512],
                            lhsT=wfc_s[buf][:, j * 128:(j + 1) * 128],
                            rhs=kcall[:, j * 1024 + hh * 512: j * 1024 + (hh + 1) * 512],
                            start=(j == 0), stop=(j == 15), skip_group_check=True)
                        tfc = b.op_inc("pe", fn) if (j == 15 and hh == 1) else b.op("pe", fn)
                if c + 2 < 64:
                    b.wait("pool", tfc)
                    t_w[c + 2] = ld("pool", wfc_s[buf], FCW[c + 2], f"dwf{tag}{buf}")
                b.wait("dve", tfc)
                b.wait("dve", square_done[buf])
                b.op("dve", lambda e, buf=buf, hidps=hidps: e.tensor_scalar_max(
                    out=hid_f32[buf][:, 0:512], in0=hidps[:, 0:512], scalar1=0.0))
                trl = b.op_inc("dve", lambda e, buf=buf, hidps=hidps: e.tensor_scalar_max(
                    out=hid_f32[buf][:, 512:1024], in0=hidps[:, 512:1024], scalar1=0.0))
                relu_done[buf] = trl
                b.wait("act", trl)
                b.wait("act", pj_done[buf])
                b.op("act", lambda e, buf=buf: e.activation(
                    out=hid_bf[buf][:, 0:512], in_=hid_f32[buf][:, 0:512], func=AF.Square))
                tsq = b.op_inc("act", lambda e, buf=buf: e.activation(
                    out=hid_bf[buf][:, 512:1024], in_=hid_f32[buf][:, 512:1024],
                    func=AF.Square))
                square_done[buf] = tsq
                if c > 0:
                    emit_pj(c - 1)
                if c == 63:
                    tpj = emit_pj(c)
            return out_evac(tpj), relu_done, square_done

        def ck_evac(tpj):
            b.wait("dve", tpj)
            return b.op_inc("dve", lambda e: e.tensor_copy(out=ck_fm[:], in_=ps_v[:]))

        t_ck, _, _ = fc_pass("k", FCK, PJK, t_kc_asm, ck_evac,
                             entry_war=(ev_done[NT - 1], t_v_done),
                             entry_pool=(v_ev[6], v_ev[7], tr_ev_done[NT - 1]))

        b.wait("dve", t_ck)
        t_vc_asm = None
        for j in range(16):
            for h in range(H):
                fn = lambda e, j=j, h=h: e.tensor_scalar(
                    out=kcall[:, j * 1024 + h * 128: j * 1024 + (h + 1) * 128],
                    in0=v_fm[:, h * 2048 + j: h * 2048 + 2048: 16],
                    scalar1=vpost[:, h * 16 + j: h * 16 + j + 1], scalar2=None,
                    op0=ALU.add)
                t_vc_asm = b.op_inc("dve", fn) if (j, h) == (15, 7) else b.op("dve", fn)
        if "KCALL" in dbg_t:
            b.wait("sp", t_vc_asm)
            ld("sp", dbg_t["KCALL"], kcall, "ddbg")

        def cv_evac(tpj):
            b.wait("dve", tpj)
            t_cvf = b.op_inc("dve", lambda e: e.tensor_copy(out=cv_fm_scr, in_=ps_v[:]))
            b.wait("pe", t_cvf)
            tev = None
            for grp in range(2):
                tgt = None
                for cc in range(4):
                    h = grp * 4 + cc
                    fn = lambda e, h=h, cc=cc: e.transpose(
                        out=ps_tr[:, cc * 128:(cc + 1) * 128],
                        in_=cv_fm_scr[:, h * 128:(h + 1) * 128], identity=idb[:])
                    tgt = b.op_inc("pe", fn) if cc == 3 else b.op("pe", fn)
                b.wait("act", tgt)
                tev = b.op_inc("act", lambda e, grp=grp: e.copy(
                    out=cv_tok[:, grp * 512:(grp + 1) * 512], in_=ps_tr[:]))
                b.wait("pe", tev)
            return tev

        t_cv, v_relu_done, v_square_done = fc_pass("v", FCV, PJV, t_vc_asm, cv_evac,
                                                   entry_war=(t_ck,),
                                                   entry_pool=(t_ck,))

        b.wait("pool", t_cv)
        ld("pool", memv_bf, MEMV, "dmsk")
        ld("pool", msm, MSMd, "dmsk")
        t_msk = ld("pool", cbm, CBMd, "dmsk")

        b.wait("pe", t_vc_asm)
        perm = [8 * (c % 16) + c // 16 for c in range(128)]
        seen = [False] * 128
        slot = 0
        t_vtok = None
        for c0 in range(128):
            if seen[c0]:
                continue
            cyc = [c0]
            seen[c0] = True
            nxt = perm[c0]
            while nxt != c0:
                cyc.append(nxt)
                seen[nxt] = True
                nxt = perm[nxt]
            tA = b.op_inc("pe", lambda e, c=cyc[0], slot=slot: e.transpose(
                out=ps_ptr[:, (slot % 2) * 128:((slot % 2) + 1) * 128],
                in_=v_tok[:, c * 128:(c + 1) * 128], identity=idb[:]))
            if len(cyc) == 1:
                b.wait("act", tA)
            slotA = slot
            slot += 1
            for k in range(1, len(cyc)):
                ckp = cyc[k]
                tB = b.op_inc("pe", lambda e, c=ckp, slot=slot: e.transpose(
                    out=ps_ptr[:, (slot % 2) * 128:((slot % 2) + 1) * 128],
                    in_=v_tok[:, c * 128:(c + 1) * 128], identity=idb[:]))
                b.wait("act", tB)
                tevv = b.op_inc("act", lambda e, c=ckp, slotA=slotA: e.copy(
                    out=v_tok[:, c * 128:(c + 1) * 128],
                    in_=ps_ptr[:, (slotA % 2) * 128:((slotA % 2) + 1) * 128]))
                b.wait("pe", tevv)
                slotA = slot
                slot += 1
            tevv = b.op_inc("act", lambda e, c=cyc[0], slotA=slotA: e.copy(
                out=v_tok[:, c * 128:(c + 1) * 128],
                in_=ps_ptr[:, (slotA % 2) * 128:((slotA % 2) + 1) * 128]))
            b.wait("pe", tevv)
            t_vtok = tevv

        if "CKFM" in dbg_t:
            b.wait("sp", t_ck)
            ld("sp", dbg_t["CKFM"], ck_fm[:], "ddbg")
        if "CVTOK" in dbg_t:
            b.wait("sp", t_cv)
            ld("sp", dbg_t["CVTOK"], cv_tok[:], "ddbg")
            b.wait("sp", t_msk)
            ld("sp", dbg_t["CVMEM"], memv_bf, "ddbg")
        if "VTOK" in dbg_t:
            b.wait("sp", t_vtok)
            ld("sp", dbg_t["VTOK"], v_tok, "ddbg")
        if phases <= 3:
            _finish(b, nc, en)
            return nc, dbg_t

        t_spill = [None] * NT
        comp_done = [None, None]
        macc_done = [None] * NT
        csim_ev = [None, None, None, None]
        pn_done = [None, None, None, None]
        ptr_ev = [None, None, None, None]
        fine_combine = None
        cproj_done = None
        att_ev = {}
        exp_sl = [None, None, None, None]
        macc_sl = [None, None, None, None]
        t_rc8p = None

        b.wait("pe", t_msk)
        b.wait("pe", t_cv)
        b.wait("pe", t_vtok)
        b.wait("pe", t_mkb)
        b.wait("pe", v_relu_done[0])
        b.wait("pe", v_relu_done[1])
        for half in range(2):
            tmx1 = b.op_inc("pe", lambda e, half=half: e.matmul(
                out=ps_misc[:, 0:512], lhsT=onesr[:],
                rhs=memv_bf[0:1, half * 512:(half + 1) * 512],
                start=True, stop=True, skip_group_check=True))
            b.wait("act", tmx1)
            tmx2 = b.op_inc("act", lambda e, half=half: e.copy(
                out=memvx[:, half * 512:(half + 1) * 512], in_=ps_misc[:, 0:512]))
            b.wait("pe", tmx2)
        t_memvx = tmx2

        class SelTile:

            def __init__(self, qt):
                self.qt = qt
                self.c = qt % 4
                self.mv = m_f32[:, qt * 128:(qt + 1) * 128]
                self.eqs = eq_scr[:, self.c * 128:(self.c + 1) * 128]
                self.mx = small[self.c][:, 5:6]
                self.it = 0
                self.stage = 0
                self.tok = None
                self.done = None
                self.started = False

            def pump(self):
                if self.done is not None:
                    return False
                if not self.started:
                    b.wait("dve", macc_done[self.qt])
                    self.started = True
                if self.stage == 0:
                    self.tok = b.op_inc("dve", lambda e, s=self: e.reduce_max(
                        out=s.mx, in_=s.mv, axis=AX.X))
                    self.stage = 1
                else:
                    b.wait("dve", self.tok)
                    b.op("dve", lambda e, s=self: e.tensor_scalar(
                        out=s.eqs, in0=s.mv, scalar1=s.mx, scalar2=None,
                        op0=ALU.is_equal))
                    b.op("dve", lambda e, s=self: e.scalar_tensor_tensor(
                        out=s.mv, in0=s.eqs, scalar=BIGSUP, in1=s.mv,
                        op0=ALU.mult, op1=ALU.add))
                    self.stage = 0
                    self.it += 1
                    if self.it == 16:
                        self.done = b.op_inc("dve", lambda e, s=self: e.tensor_scalar(
                            out=sel_bf[:, s.c * 128:(s.c + 1) * 128], in0=s.mv,
                            scalar1=SELTH, scalar2=None, op0=ALU.is_lt))
                return True

        for g in range(NG):
            sel_tiles = []

            def pump_sel(n):
                for _ in range(n):
                    for stl in sel_tiles:
                        if stl.pump():
                            break

            for qi in range(4):
                qt = 4 * g + qi
                tcomp = None
                pendPT = None
                pendPV = None

                def flush_pt(pend):
                    hp, hbp, t_exp_p = pend
                    b.wait("pe", t_exp_p)
                    b.wait("pe", ptr_ev[hbp])
                    tpt = b.op_inc("pe", lambda e, hbp=hbp: e.transpose(
                        out=ps_tr[:, hbp * 128:(hbp + 1) * 128],
                        in_=pn_sb[hbp][:, 1:129], identity=idb[:]))
                    b.wait("act", tpt)
                    tpe = b.op_inc("act", lambda e, hbp=hbp: e.copy(
                        out=pt_sb[hbp][:], in_=ps_tr[:, hbp * 128:(hbp + 1) * 128]))
                    ptr_ev[hbp] = tpe
                    return (hp, hbp, tpe)

                def flush_pv4(pend):
                    hp, hbp, tpe = pend
                    b.wait("pe", tpe)
                    if hp == 0:
                        b.wait("pe", comp_done[0])
                    return b.op_inc("pe", lambda e, hp=hp, hbp=hbp: e.matmul(
                        out=ps_v[:, hp * 128:(hp + 1) * 128], lhsT=pt_sb[hbp][:],
                        rhs=cv_tok[:, hp * 128:(hp + 1) * 128],
                        start=(hp in (0, 4)), stop=(hp in (3, 7)),
                        skip_group_check=True))

                for h in range(H):
                    hb = h % 4
                    cps = ps_q[:, hb * 512: hb * 512 + 129] if hb < 2 else \
                        ps_k[:, (hb - 2) * 512: (hb - 2) * 512 + 129]
                    b.wait("pe", csim_ev[hb])
                    b.op("pe", lambda e, h=h, qt=qt, cps=cps: e.matmul(
                        out=cps[:, 0:1],
                        lhsT=q_fm[:, h * 2048 + qt * 128: h * 2048 + (qt + 1) * 128],
                        rhs=memkt_bf[:, h:h + 1],
                        start=True, stop=True, skip_group_check=True))
                    tmm = b.op_inc("pe", lambda e, h=h, qt=qt, cps=cps: e.matmul(
                        out=cps[:, 1:129],
                        lhsT=q_fm[:, h * 2048 + qt * 128: h * 2048 + (qt + 1) * 128],
                        rhs=ck_fm[:, h * 128:(h + 1) * 128],
                        start=False, stop=True, skip_group_check=True))
                    b.wait("dve", tmm)
                    b.wait("dve", exp_sl[hb])
                    b.wait("dve", macc_sl[hb])
                    tcs = b.op_inc("dve", lambda e, hb=hb, qt=qt, cps=cps: e.scalar_tensor_tensor(
                        out=csim_sb[hb][:], in0=cps, scalar=SCALE,
                        in1=cmaster[:, 120 - 8 * qt: 249 - 8 * qt],
                        op0=ALU.mult, op1=ALU.add))
                    csim_ev[hb] = tcs
                    b.wait("pool", tcs)
                    fn = lambda e, hb=hb, qt=qt: e.tensor_copy(
                        out=m_f32[:, qt * 128:(qt + 1) * 128], in_=csim_sb[hb][:, 1:129])
                    fn2 = lambda e, hb=hb, qt=qt: e.tensor_tensor(
                        out=m_f32[:, qt * 128:(qt + 1) * 128],
                        in0=m_f32[:, qt * 128:(qt + 1) * 128],
                        in1=csim_sb[hb][:, 1:129], op=ALU.add)
                    tma = b.op_inc("pool", fn if h == 0 else fn2)
                    macc_sl[hb] = tma
                    if h == H - 1:
                        macc_done[qt] = tma
                    b.wait("act", tcs)
                    b.wait("act", pn_done[hb])
                    if h == 0:
                        b.wait("act", t_rc8p)
                    t_exp = b.op_inc("act", lambda e, hb=hb, h=h: e.activation(
                        out=pn_sb[hb][:], in_=csim_sb[hb][:], func=AF.Exp,
                        accum_out=qtab[:, h:h + 1]))
                    exp_sl[hb] = t_exp
                    b.wait("dve", t_exp)
                    t_fc = b.op_inc("dve", lambda e, hb=hb, h=h: e.tensor_copy(
                        out=qtab[:, 8 + h: 9 + h], in_=pn_sb[hb][:, 0:1]))
                    pn_done[hb] = t_fc
                    nextPV = flush_pt(pendPT) if pendPT is not None else None
                    if pendPV is not None:
                        tcomp = flush_pv4(pendPV)
                    pendPV = nextPV
                    pendPT = (h, hb, t_exp)
                    pump_sel(4)
                nextPV = flush_pt(pendPT)
                if pendPV is not None:
                    tcomp = flush_pv4(pendPV)
                tcomp = flush_pv4(nextPV)
                pendPT = pendPV = None
                b.wait("dve", t_exp)
                t_rc8 = b.op_inc("dve", lambda e: e.reciprocal(
                    out=qtab[:, 16:24], in_=qtab[:, 0:8]))
                t_rc8p = t_rc8
                b.wait("dve", t_rc8)
                t_sc8 = b.op_inc("dve", lambda e, qt=qt: e.tensor_tensor(
                    out=qtab[:, 24:32], in0=qtab[:, 16:24],
                    in1=strat_sig[:, qt * 24: qt * 24 + 24: 3], op=ALU.mult))
                b.wait("dve", t_sc8)
                t_ms8 = b.op_inc("dve", lambda e: e.tensor_tensor(
                    out=memsc[:, 0:8], in0=qtab[:, 8:16], in1=qtab[:, 24:32],
                    op=ALU.mult))
                b.wait("dve", tcomp)
                if "ATT" in dbg_t:
                    b.wait("dve", att_ev.get(g - 1))
                b.wait("dve", t_ms8)
                for h in range(H):
                    b.op("dve", lambda e, h=h, qi=qi: e.tensor_scalar(
                        out=att_acc[:, qi * 1024 + h * 128: qi * 1024 + (h + 1) * 128],
                        in0=ps_v[:, h * 128:(h + 1) * 128],
                        scalar1=qtab[:, 24 + h: 25 + h], scalar2=None, op0=ALU.mult))
                b.wait("dve", t_memvx)
                tmem = None
                for h in range(H):
                    fn = lambda e, h=h, qi=qi: e.scalar_tensor_tensor(
                        out=att_acc[:, qi * 1024 + h * 128: qi * 1024 + (h + 1) * 128],
                        in0=memvx[:, h * 128:(h + 1) * 128], scalar=memsc[:, h:h + 1],
                        in1=att_acc[:, qi * 1024 + h * 128: qi * 1024 + (h + 1) * 128],
                        op0=ALU.mult, op1=ALU.add)
                    tmem = b.op_inc("dve", fn) if h == H - 1 else b.op("dve", fn)
                comp_done[0] = comp_done[1] = tmem
                sel_tiles.append(SelTile(qt))

            while any(stl.done is None for stl in sel_tiles):
                pump_sel(1)
            for stl in sel_tiles:
                b.wait("pe", stl.done)
            for tk in ptr_ev:
                b.wait("pe", tk)
            tgt = None
            for cc in range(4):
                fn = lambda e, cc=cc: e.transpose(
                    out=ps_tr[:, cc * 128:(cc + 1) * 128],
                    in_=sel_bf[:, cc * 128:(cc + 1) * 128], identity=idb[:])
                tgt = b.op_inc("pe", fn) if cc == 3 else b.op("pe", fn)
            b.wait("act", tgt)
            t_selt = b.op_inc("act", lambda e: e.copy(out=selt, in_=ps_tr[:]))
            b.wait("pe", t_selt)
            if "SELT" in dbg_t:
                b.wait("sp", t_selt)
                att_ev[(g, "selt")] = ld("sp", dbg_t["SELT"][g], selt, "ddbg")
                b.wait("dve", att_ev[(g, "selt")])
                b.wait("pe", att_ev[(g, "selt")])

            kmax = 4 * g + 3
            mf_done = [None] * 16
            b.wait("dve", t_selt)
            t_s250 = b.op_inc("dve", lambda e: e.tensor_scalar(
                out=selt250, in0=selt, scalar1=1.0, scalar2=250000.0,
                op0=ALU.subtract, op1=ALU.mult))
            if g > 0:
                b.wait("pe", fine_combine)
            b.wait("pe", comp_done[0])
            for kt in range((max(0, 4 * g - 1)) if FADD else 0, kmax + 1):
                dp = g * 512 - kt * 128
                tsx = b.op_inc("pe", lambda e, kt=kt: e.matmul(
                    out=ps_v[:, 0:512], lhsT=efull[:, kt * 128:(kt + 1) * 128],
                    rhs=selt, start=True, stop=True, skip_group_check=True))
                if dp >= 128:
                    b.wait("act", tsx)
                    tmf = b.op_inc("act", lambda e, kt=kt: e.copy(
                        out=mf01[:, kt * 512:(kt + 1) * 512], in_=ps_v[:, 0:512]))
                    b.wait("pe", tmf)
                else:
                    b.wait("dve", tsx)
                    b.op("dve", lambda e, dp=dp: e.tensor_tensor(
                        out=mfd_scr, in0=ps_v[:, 0:512],
                        in1=cmm[:, dp + 384: dp + 384 + 512], op=ALU.mult))
                    tmf = b.op_inc("dve", lambda e, kt=kt, dp=dp: e.tensor_tensor(
                        out=mf01[:, kt * 512:(kt + 1) * 512], in0=mfd_scr,
                        in1=cbm[:, dp + 384: dp + 384 + 512], op=ALU.max))
                    b.wait("pe", tmf)
                mf_done[kt] = tmf
            b.wait("pe", t_s250)

            exp_done = [None, None]
            pv_done3 = [None, None, None]
            prev_evac = [None]
            pending = []
            for tk in csim_ev:
                b.wait("pe", tk)
            for stl in sel_tiles:
                b.wait("act", stl.done)
            b.wait("act", t_spill[4 * g - 1] if g > 0 else None)
            npair = (kmax + 1) // 2
            e2x = [e2[0], e2[1], e2c]
            pf2x = [pf2[0], pf2[1], pf2c]
            psl2x = [psl2[0], psl2[1], pslc]
            head_last_pv = [None] * H

            def flush_pv():
                h, p, p3, tE, tPF = pending.pop(0)
                flags = head_flags[h]
                b.wait("pe", tPF if tPF is not None else tE)
                if tPF is not None:
                    b.wait("pe", tE)
                if flags[0]:
                    b.wait("pe", prev_evac[0])
                last = None
                for ktip in range(2):
                    kt = 2 * p + ktip
                    dp = g * 512 - kt * 128
                    fadd = FADD and dp > 128
                    slide = -384 <= dp <= 128
                    src = e2x[p3] if fadd else pf2x[p3]
                    for c in range(4):
                        b.op("pe", lambda e, src=src, ktip=ktip, c=c, kt=kt, h=h,
                             ff=flags[0]: e.matmul(
                            out=ps_k[:, c * 128:(c + 1) * 128],
                            lhsT=src[:, ktip * 512 + c * 128: ktip * 512 + (c + 1) * 128],
                            rhs=v_tok[:, kt * 1024 + h * 128: kt * 1024 + (h + 1) * 128],
                            start=ff, stop=False, skip_group_check=True))
                        last = b.op_inc("pe", lambda e, src=src, ktip=ktip, c=c,
                                        sf=flags[2]: e.matmul(
                            out=ps_sums[:, c:c + 1],
                            lhsT=src[:, ktip * 512 + c * 128: ktip * 512 + (c + 1) * 128],
                            rhs=onesb[:], start=sf, stop=False, skip_group_check=True))
                        if slide:
                            b.op("pe", lambda e, p3=p3, ktip=ktip, c=c, kt=kt, h=h,
                                 sf2=flags[1]: e.matmul(
                                out=ps_k[:, 512 + c * 128: 512 + (c + 1) * 128],
                                lhsT=psl2x[p3][:, ktip * 512 + c * 128:
                                               ktip * 512 + (c + 1) * 128],
                                rhs=v_tok[:, kt * 1024 + h * 128: kt * 1024 + (h + 1) * 128],
                                start=sf2, stop=False, skip_group_check=True))
                            last = b.op_inc("pe", lambda e, p3=p3, ktip=ktip, c=c: e.matmul(
                                out=ps_sums[:, 4 + c: 5 + c],
                                lhsT=psl2x[p3][:, ktip * 512 + c * 128:
                                               ktip * 512 + (c + 1) * 128],
                                rhs=onesb[:], start=False, stop=False,
                                skip_group_check=True))
                            flags[1] = False
                        flags[0] = False
                        flags[2] = False
                pv_done3[p3] = last
                if p == npair - 1:
                    head_last_pv[h] = last
                    fc = emit_head_tail(h)
                    return last, fc
                return last, None

            def emit_head_tail(h):
                b.wait("act", head_last_pv[h])
                b.op("act", lambda e: e.copy(out=pv_ev, in_=ps_k[:]))
                tev = b.op_inc("act", lambda e: e.copy(
                    out=sums_ev[:, 0:8], in_=ps_sums[:]))
                prev_evac[0] = tev
                b.wait("dve", tev)
                b.wait("dve", comp_done[0])
                t_rc = b.op_inc("dve", lambda e: e.reciprocal(
                    out=small[2][:, 0:8], in_=sums_ev[:, 0:8]))
                b.wait("dve", t_rc)
                t_sc = None
                for c in range(4):
                    qt = 4 * g + c
                    b.op("dve", lambda e, c=c, qt=qt, h=h: e.tensor_tensor(
                        out=small[3][:, c:c + 1], in0=small[2][:, c:c + 1],
                        in1=strat_sig[:, qt * 24 + 3 * h + 1: qt * 24 + 3 * h + 2],
                        op=ALU.mult))
                    fn = lambda e, c=c, qt=qt, h=h: e.tensor_tensor(
                        out=small[3][:, 4 + c: 5 + c], in0=small[2][:, 4 + c: 5 + c],
                        in1=strat_sig[:, qt * 24 + 3 * h + 2: qt * 24 + 3 * h + 3],
                        op=ALU.mult)
                    t_sc = b.op_inc("dve", fn) if c == 3 else b.op("dve", fn)
                b.wait("dve", t_sc)
                tcb = None
                for c in range(4):
                    b.op("dve", lambda e, c=c, h=h: e.scalar_tensor_tensor(
                        out=att_acc[:, c * 1024 + h * 128: c * 1024 + (h + 1) * 128],
                        in0=pv_ev[:, c * 128:(c + 1) * 128], scalar=small[3][:, c:c + 1],
                        in1=att_acc[:, c * 1024 + h * 128: c * 1024 + (h + 1) * 128],
                        op0=ALU.mult, op1=ALU.add))
                for c in range(4):
                    fn = lambda e, c=c, h=h: e.scalar_tensor_tensor(
                        out=att_acc[:, c * 1024 + h * 128: c * 1024 + (h + 1) * 128],
                        in0=pv_ev[:, 512 + c * 128: 512 + (c + 1) * 128],
                        scalar=small[3][:, 4 + c: 5 + c],
                        in1=att_acc[:, c * 1024 + h * 128: c * 1024 + (h + 1) * 128],
                        op0=ALU.mult, op1=ALU.add)
                    tcb = b.op_inc("dve", fn) if c == 3 else b.op("dve", fn)
                return tcb

            gp = 0
            head_flags = {}
            for h in range(H):
                head_flags[h] = [True, True, True]
                for p in range(npair):
                    kt0 = 2 * p
                    pp = gp % 2
                    p3 = gp % P3N
                    Sps = ps_q if pp == 0 else ps_v
                    b.wait("pe", exp_done[pp])
                    if h == 0 and p <= 1:
                        b.wait("pe", comp_done[0])
                        b.wait("pe", mf_done[kmax])
                    tS = None
                    for ktip in range(2):
                        kt = kt0 + ktip
                        fadd = FADD and g * 512 - kt * 128 > 128
                        fnS = lambda e, h=h, kt=kt, g=g, Sps=Sps, ktip=ktip, fadd=fadd: \
                            e.matmul(
                                out=Sps[:, ktip * 512:(ktip + 1) * 512],
                                lhsT=k_fm[:, h * 2048 + kt * 128:
                                          h * 2048 + (kt + 1) * 128],
                                rhs=q_fm[:, h * 2048 + g * 512:
                                         h * 2048 + (g + 1) * 512],
                                start=True, stop=not fadd, skip_group_check=True)
                        if ktip == 1 and not fadd:
                            tS = b.op_inc("pe", fnS)
                        else:
                            b.op("pe", fnS)
                        if fadd:
                            fnM = lambda e, kt=kt, Sps=Sps, ktip=ktip: e.matmul(
                                out=Sps[:, ktip * 512:(ktip + 1) * 512],
                                lhsT=efull[:, kt * 128:(kt + 1) * 128],
                                rhs=selt250, start=False, stop=True,
                                skip_group_check=True)
                            tS = b.op_inc("pe", fnM) if ktip == 1 else b.op("pe", fnM)
                    if len(pending) >= max(1, PV_DEFER):
                        _, fcn = flush_pv()
                        if fcn is not None:
                            fine_combine = fcn
                    b.wait("act", tS)
                    b.wait("act", pv_done3[p3])
                    tE = b.op_inc("act", lambda e, p3=p3, Sps=Sps: e.activation(
                        out=e2x[p3], in_=Sps[:, 0:1024], func=AF.Exp, scale=SCALE))
                    exp_done[pp] = tE
                    tPF = None
                    halves = [(ktip, g * 512 - (kt0 + ktip) * 128) for ktip in range(2)]
                    if any(dp <= 128 for _, dp in halves) or not FADD:
                        b.wait("dve", tE)
                        b.wait("dve", pv_done3[p3])
                        for ktip, dp in halves:
                            if dp <= 128 or not FADD:
                                b.wait("dve", mf_done[kt0 + ktip])
                                tPF = b.op_inc("dve", lambda e, p3=p3, ktip=ktip,
                                               kt0=kt0: e.tensor_tensor(
                                    out=pf2x[p3][:, ktip * 512:(ktip + 1) * 512],
                                    in0=e2x[p3][:, ktip * 512:(ktip + 1) * 512],
                                    in1=mf01[:, (kt0 + ktip) * 512:
                                             (kt0 + ktip + 1) * 512], op=ALU.mult))
                            if -384 <= dp <= 128:
                                tPF = b.op_inc("dve", lambda e, p3=p3, ktip=ktip,
                                               dp=dp: e.tensor_tensor(
                                    out=psl2x[p3][:, ktip * 512:(ktip + 1) * 512],
                                    in0=e2x[p3][:, ktip * 512:(ktip + 1) * 512],
                                    in1=msm[:, dp + 384: dp + 384 + 512], op=ALU.mult))
                    pending.append((h, p, p3, tE, tPF))
                    gp += 1
                    if PV_DEFER == 0:
                        _, fcn = flush_pv()
                        if fcn is not None:
                            fine_combine = fcn
            while pending:
                _, fcn = flush_pv()
                if fcn is not None:
                    fine_combine = fcn

            if "ATT" in dbg_t:
                b.wait("sp", fine_combine)
                att_ev[g] = ld("sp", dbg_t["ATT"][g], att_acc, "ddbg")

            for c in range(4):
                qt = 4 * g + c
                b.wait("pe", fine_combine)
                b.wait("pe", cproj_done)
                tev = None
                for grp in range(2):
                    tgt = None
                    for cc in range(4):
                        ci = grp * 4 + cc
                        fn = lambda e, c=c, ci=ci, cc=cc: e.transpose(
                            out=ps_tr[:, cc * 128:(cc + 1) * 128],
                            in_=att_acc[:, c * 1024 + ci * 128: c * 1024 + (ci + 1) * 128],
                            identity=idb[:])
                        tgt = b.op_inc("pe", fn) if cc == 3 else b.op("pe", fn)
                    b.wait("act", tgt)
                    tev = b.op_inc("act", lambda e, grp=grp: e.copy(
                        out=attT[:, grp * 512:(grp + 1) * 512], in_=ps_tr[:]))
                    b.wait("pe", tev)
                b.wait("pe", tev)
                if g == 0 and c == 0:
                    b.wait("pe", t_cpw)
                b.wait("pe", comp_done[0])
                tcp = None
                for ci in range(8):
                    for hh in range(2):
                        fn = lambda e, ci=ci, hh=hh: e.matmul(
                            out=ps_v[:, hh * 512:(hh + 1) * 512],
                            lhsT=attT[:, ci * 128:(ci + 1) * 128],
                            rhs=cpw_sb[:, ci * 1024 + hh * 512: ci * 1024 + (hh + 1) * 512],
                            start=(ci == 0), stop=(ci == 7), skip_group_check=True)
                        tcp = b.op_inc("pe", fn) if (ci == 7 and hh == 1) else b.op("pe", fn)
                cproj_done = tcp
                b.wait("dve", tcp)
                if qt >= 1:
                    b.wait("dve", t_spill[qt - 1])
                tso = b.op_inc("dve", lambda e: e.tensor_copy(
                    out=spill_ev[:, 0:1024], in_=ps_v[:]))
                comp_done[0] = comp_done[1] = tso
                b.wait("sp", tso)
                t_spill[qt] = ld("sp", X1SPILL[qt * 128:(qt + 1) * 128, :],
                                 spill_ev[:, 0:1024], f"dsp{qt % 2}")

        b.wait("pool", cproj_done)
        t_wrt = [None] * 8
        for ci in range(8):
            t_wrt[ci] = ld("pool", wrt_sb[:, ci * 1024:(ci + 1) * 1024], WRT[ci], "dwrt")
        b.wait("pool", fine_combine)
        t_wvt = [None] * 8
        for ci in range(8):
            t_wvt[ci] = ld("pool", wvt6a[:, ci * 1024:(ci + 1) * 1024], WVT[ci], "dwvt")

        if "X1" in dbg_t:
            for qt in range(NT):
                b.wait("sp", t_spill[qt])
            ld("sp", dbg_t["X1"], X1SPILL, "ddbg")
        if phases <= 5:
            _finish(b, nc, en)
            return nc, dbg_t

        yv = y_fm.rearrange("p (c n) -> p c n", n=2049)
        b.wait("dve", fine_combine)
        t_yz = b.op_inc("dve", lambda e: e.memset(yv[:, :, 0:1], 0.0))
        t_x1 = [None] * NT
        t_x6 = [None] * NT
        b.wait("sp", t_spill[0])
        t_x1[0] = ld("sp", x1_ld[0][:, 0:1024], X1SPILL[0:128, :], "dx1a")
        b.wait("sp", t_spill[1])
        t_x1[1] = ld("sp", x1_ld[1][:, 0:1024], X1SPILL[128:256, :], "dx1b")
        t_x6[0] = ld("sp", x6_ld[0][:, 0:1024], X[0:128, :], "dx6a")
        t_x6[1] = ld("sp", x6_ld[1][:, 0:1024], X[128:256, :], "dx6b")

        yev_done = [None] * NT
        rk_done = [None] * NT
        rsig_done = [None] * NT
        karelu_done = [None] * NT
        kasq_done = [None] * NT
        katr_done = [None] * NT
        vout_done = [None] * NT
        xout_done = [None] * NT
        x1r_done = [None] * NT
        t_out = [None] * NT

        def emit_vout(qt):
            b.wait("pe", kasq_done[qt])
            if qt == 0:
                b.wait("pe", comp_done[0])
            tev = None
            for grp in range(2):
                tgt = None
                for cc in range(4):
                    ci = grp * 4 + cc
                    fn = lambda e, ci=ci, cc=cc: e.transpose(
                        out=ps_tr[:, cc * 128:(cc + 1) * 128],
                        in_=ka_tok[:, ci * 128:(ci + 1) * 128], identity=idb[:])
                    tgt = b.op_inc("pe", fn) if cc == 3 else b.op("pe", fn)
                b.wait("act", tgt)
                out_ap = kaT.rearrange("p (c t) -> p c t", t=2048)[
                    :, grp * 4:(grp + 1) * 4, qt * 128:(qt + 1) * 128]
                tev = b.op_inc("act", lambda e, o=out_ap: e.copy(
                    out=o, in_=ps_tr[:].rearrange("p (c t) -> p c t", t=128)))
                b.wait("pe", tev)
            katr_done[qt] = tev
            if qt >= 1:
                b.wait("pe", xout_done[qt - 1])
            tvo = None
            for ci in range(8):
                if qt == 0 and ci == 0:
                    for cj in range(8):
                        b.wait("pe", t_wvt[cj])
                for hh in range(2):
                    fn = lambda e, qt=qt, ci=ci, hh=hh: e.matmul(
                        out=ps_v[:, hh * 512:(hh + 1) * 512],
                        lhsT=kaT[:, ci * 2048 + qt * 128: ci * 2048 + (qt + 1) * 128],
                        rhs=wvt6a[:, ci * 1024 + hh * 512: ci * 1024 + (hh + 1) * 512],
                        start=(ci == 0), stop=(ci == 7), skip_group_check=True)
                    tvo = b.op_inc("pe", fn) if (ci == 7 and hh == 1) else b.op("pe", fn)
            vout_done[qt] = tvo
            b.wait("dve", tvo)
            b.wait("dve", rsig_done[qt])
            if qt >= 1:
                b.wait("dve", t_out[qt - 1])
            ob = qt % 2
            b.op("dve", lambda e: e.tensor_tensor(
                out=xout_sb[:, 0:512], in0=ps_v[:, 0:512], in1=r_sig[:, 0:512],
                op=ALU.mult))
            b.op("dve", lambda e: e.tensor_tensor(
                out=xout_sb[:, 512:1024], in0=ps_v[:, 512:1024], in1=r_sig[:, 512:1024],
                op=ALU.mult))
            b.op("dve", lambda e, ob=ob: e.tensor_tensor(
                out=xout_sb[:, 0:512], in0=xout_sb[:, 0:512],
                in1=x1_ld[ob][:, 0:512], op=ALU.add))
            txo = b.op_inc("dve", lambda e, ob=ob: e.tensor_tensor(
                out=xout_sb[:, 512:1024], in0=xout_sb[:, 512:1024],
                in1=x1_ld[ob][:, 512:1024], op=ALU.add))
            xout_done[qt] = txo
            b.wait("sp", txo)
            t_out[qt] = ld("sp", XOUT[qt * 128:(qt + 1) * 128, :], xout_sb[:, 0:1024],
                           f"do{qt % 2}")

        t_ln2_tab = [None] * NT

        def emit_ln2(qt):
            buf = qt % 2
            st = stat_scr[buf]
            yt = y_tok2[buf]
            b.wait("dve", t_x1[qt])
            b.wait("dve", t_x6[qt])
            if qt >= 2:
                b.wait("dve", xout_done[qt - 2])
            tx1r = b.op_inc("dve", lambda e, buf=buf: e.tensor_tensor(
                out=x1_ld[buf][:, 0:1024], in0=x1_ld[buf][:, 0:1024],
                in1=x6_ld[buf][:, 0:1024], op=ALU.add))
            x1r_done[qt] = tx1r
            b.wait("act", tx1r)
            if qt == 1:
                b.wait("act", t_spill[NT - 1])
            t_ssq = b.op_inc("act", lambda e, buf=buf, st=st, yt=yt: e.activation(
                out=yt, in_=x1_ld[buf][:, 0:1024], func=AF.Square, scale=32.0,
                accum_out=st[:, 1:2]))
            t_sum = b.op_inc("dve", lambda e, buf=buf, st=st: e.reduce_sum(
                out=st[:, 0:1], in_=x1_ld[buf][:, 0:1024], axis=AX.X))
            b.wait("dve", t_sum)
            b.wait("dve", t_ssq)
            t_u = b.op_inc("dve", lambda e, st=st: e.scalar_tensor_tensor(
                out=st[:, 4:5], in0=st[:, 0:1], scalar=st[:, 0:1], in1=st[:, 1:2],
                op0=ALU.mult, op1=ALU.subtract))
            b.op("dve", lambda e, st=st: e.tensor_scalar(
                out=st[:, 2:3], in0=st[:, 0:1], scalar1=OD, scalar2=None, op0=ALU.mult))
            b.wait("act", t_u)
            t_ln5 = b.op_inc("act", lambda e, st=st: e.activation(
                out=st[:, 5:6], in_=st[:, 4:5], func=AF.Ln, scale=-1.0,
                bias=epsT[:, 2:3]))
            b.wait("act", t_ln5)
            t_rstd = b.op_inc("act", lambda e, st=st: e.activation(
                out=st[:, 6:7], in_=st[:, 5:6], func=AF.Exp, scale=-0.5,
                bias=epsT[:, 3:4]))
            b.wait("dve", t_rstd)
            if qt == 0:
                b.wait("dve", t_yz)
            t_ln2_tab[qt] = b.op_inc("dve", lambda e, buf=buf, st=st, yt=yt:
                                     e.scalar_tensor_tensor(
                out=yt, in0=x1_ld[buf][:, 0:1024], scalar=st[:, 2:3],
                in1=st[:, 6:7].broadcast_to([128, 1024]),
                op0=ALU.subtract, op1=ALU.mult))

        emit_ln2(0)
        for qt in range(NT):
            buf = qt % 2
            b.wait("pe", t_ln2_tab[qt])
            tev = None
            for grp in range(2):
                tgt = None
                for cc in range(4):
                    ci = grp * 4 + cc
                    fn = lambda e, buf=buf, ci=ci, cc=cc: e.transpose(
                        out=ps_tr[:, cc * 128:(cc + 1) * 128],
                        in_=y_tok2[buf][:, ci * 128:(ci + 1) * 128], identity=idb[:])
                    tgt = b.op_inc("pe", fn) if cc == 3 else b.op("pe", fn)
                b.wait("act", tgt)
                out_ap = yv[:, grp * 4:(grp + 1) * 4, 1 + qt * 128: 1 + (qt + 1) * 128]
                tev = b.op_inc("act", lambda e, o=out_ap: e.copy(
                    out=o, in_=ps_tr[:].rearrange("p (c t) -> p c t", t=128)))
                b.wait("pe", tev)
            yev_done[qt] = tev

            b.wait("dve", tev)
            if qt >= 2:
                b.wait("dve", rk_done[qt - 2])
            tmx = None
            ypre_v = yv[:, :, qt * 128:(qt + 1) * 128]
            ycur_v = yv[:, :, 1 + qt * 128: 1 + (qt + 1) * 128]
            for s, mu, omu, dst in ((0, mur_sb, omur_sb, xr_sc2[buf]),
                                    (1, muk_sb, omuk_sb, xk_sc2[buf])):
                dv = dst.rearrange("p (c n) -> p c n", n=128)
                mub = mu[:].unsqueeze(2).broadcast_to([128, 8, 128])
                omub = omu[:].unsqueeze(2).broadcast_to([128, 8, 128])
                b.op("dve", lambda e, dv=dv, omub=omub, ypre_v=ypre_v: e.tensor_tensor(
                    out=dv, in0=ypre_v, in1=omub, op=ALU.mult))
                b.op("dve", lambda e, mub=mub, ycur_v=ycur_v: e.tensor_tensor(
                    out=mix_tmp.rearrange("p (c n) -> p c n", n=128), in0=ycur_v,
                    in1=mub, op=ALU.mult))
                fn = lambda e, dst=dst: e.tensor_tensor(
                    out=dst, in0=dst, in1=mix_tmp, op=ALU.add)
                tmx = b.op_inc("dve", fn) if s == 1 else b.op("dve", fn)

            b.wait("pe", tmx)
            if qt == 0:
                for cj in range(8):
                    b.wait("pe", t_wrt[cj])
                b.wait("pe", t_wkt)
            if qt >= 1:
                b.wait("pe", rsig_done[qt - 1])
                b.wait("pe", karelu_done[qt - 1])
            trk = None
            for ci in range(8):
                for hh in range(2):
                    b.op("pe", lambda e, buf=buf, ci=ci, hh=hh: e.matmul(
                        out=ps_q[:, hh * 512:(hh + 1) * 512],
                        lhsT=xr_sc2[buf][:, ci * 128:(ci + 1) * 128],
                        rhs=wrt_sb[:, ci * 1024 + hh * 512: ci * 1024 + (hh + 1) * 512],
                        start=(ci == 0), stop=(ci == 7), skip_group_check=True))
                    fn = lambda e, buf=buf, ci=ci, hh=hh: e.matmul(
                        out=ps_k[:, hh * 512:(hh + 1) * 512],
                        lhsT=xk_sc2[buf][:, ci * 128:(ci + 1) * 128],
                        rhs=wkt_sb[:, ci * 1024 + hh * 512: ci * 1024 + (hh + 1) * 512],
                        start=(ci == 0), stop=(ci == 7), skip_group_check=True)
                    trk = b.op_inc("pe", fn) if (ci == 7 and hh == 1) else b.op("pe", fn)
            rk_done[qt] = trk
            if qt >= 1:
                emit_vout(qt - 1)
                if qt + 1 < NT:
                    nb = (qt + 1) % 2
                    b.wait("sp", t_spill[qt + 1])
                    b.wait("sp", xout_done[qt - 1])
                    b.wait("sp", x1r_done[qt - 1])
                    sem = "dx1a" if nb == 0 else "dx1b"
                    t_x1[qt + 1] = ld("sp", x1_ld[nb][:, 0:1024],
                                      X1SPILL[(qt + 1) * 128:(qt + 2) * 128, :], sem)
                    sem6 = "dx6a" if nb == 0 else "dx6b"
                    t_x6[qt + 1] = ld("sp", x6_ld[nb][:, 0:1024],
                                      X[(qt + 1) * 128:(qt + 2) * 128, :], sem6)

            b.wait("act", trk)
            if qt >= 1:
                b.wait("act", xout_done[qt - 1])
            trs = b.op_inc("act", lambda e: e.activation(
                out=r_sig[:], in_=ps_q[:], func=AF.Sigmoid))
            rsig_done[qt] = trs
            b.wait("dve", trk)
            if qt >= 1:
                b.wait("dve", kasq_done[qt - 1])
            trl = b.op_inc("dve", lambda e: e.tensor_scalar_max(
                out=k2relu[:, 0:1024], in0=ps_k[:], scalar1=0.0))
            karelu_done[qt] = trl
            b.wait("act", trl)
            if qt >= 1:
                b.wait("act", katr_done[qt - 1])
            tks = b.op_inc("act", lambda e: e.activation(
                out=ka_tok[:], in_=k2relu[:, 0:1024], func=AF.Square))
            kasq_done[qt] = tks
            if qt + 1 < NT:
                emit_ln2(qt + 1)

        emit_vout(NT - 1)
        if "YFM" in dbg_t:
            b.wait("sp", yev_done[NT - 1])
            ld("sp", dbg_t["YFM"], y_fm, "ddbg")
        if "KAT" in dbg_t:
            b.wait("sp", katr_done[NT - 1])
            ld("sp", dbg_t["KAT"], kaT, "ddbg")
        if "RSIG15" in dbg_t:
            b.wait("sp", rsig_done[NT - 1])
            ld("sp", dbg_t["RSIG15"], r_sig[:], "ddbg")
        if "X1LD15" in dbg_t:
            b.wait("sp", x1r_done[NT - 1])
            ld("sp", dbg_t["X1LD15"], x1_ld[(NT - 1) % 2][:, 0:1024], "ddbg")
        _finish(b, nc, en)
    return nc, dbg_t


def _finish(b, nc, en):
    sems = {}
    for name in sorted(b.sem_names):
        sems[name] = en(nc.semaphore(name))
    for name, val in sorted(b.dma_cnt.items()):
        b.q["sp"].append(("wait", name, val))
    block = en(nc.Block())

    @block.tensor
    def _(tensor):
        b.replay("pe", tensor, sems)

    @block.scalar
    def _(scalar):
        b.replay("act", scalar, sems)

    @block.vector
    def _(vector):
        b.replay("dve", vector, sems)

    @block.gpsimd
    def _(gpsimd):
        b.replay("pool", gpsimd, sems)

    @block.sync
    def _(sync):
        b.replay("sp", sync, sems)


def kernel(**inputs):
    in_maps = host_prepare(inputs)
    nc, _ = build_program()
    res = run_bass_kernel_spmd(nc, in_maps, core_ids=list(range(8)))
    return np.stack([res.results[c]["XOUT"] for c in range(8)], 0)


# version 09 (final)
# speedup vs baseline: 1.0414x; 1.0414x over previous
import numpy as np
import ml_dtypes
from contextlib import ExitStack

import concourse.bass as bass
import concourse.mybir as mybir
from concourse.bass_utils import run_bass_kernel_spmd

F32 = mybir.dt.float32
BF16 = mybir.dt.bfloat16
FP8 = mybir.dt.float8e4
AF = mybir.ActivationFunctionType
FADD = True
PV_DEFER = 2
P3N = 3
ALU = mybir.AluOpType
AX = mybir.AxisListType
bfd = ml_dtypes.bfloat16
f8d = ml_dtypes.float8_e4m3fn

T = 2048
DM = 1024
H = 8
D = 128
BS = 16
WIN = 128
SCALE = 0.12
NT = 16
NG = 4
NEG = -30000.0
BIGSUP = -1e30
SELTH = -1e29


class Build:
    ENGINES = ("pe", "act", "dve", "pool", "sp")

    def __init__(self):
        self.q = {e: [] for e in self.ENGINES}
        self.cnt = {e: 0 for e in self.ENGINES}
        self.dma_cnt = {}
        self.sem_names = set(self.ENGINES)

    def op(self, eng, fn):
        self.q[eng].append(("op", fn, None))
        return None

    def op_inc(self, eng, fn):
        self.cnt[eng] += 1
        self.q[eng].append(("op", fn, eng))
        return (eng, self.cnt[eng])

    def dma(self, queue, fn, sem):
        self.sem_names.add(sem)
        self.dma_cnt[sem] = self.dma_cnt.get(sem, 0) + 16
        self.q[queue].append(("op", fn, sem))
        return (sem, self.dma_cnt[sem])

    def wait(self, eng, token):
        if token is None:
            return
        sem_name, val = token
        if val <= 0:
            return
        self.q[eng].append(("wait", sem_name, val))

    def replay(self, eng_name, eng_obj, sems):
        for kind, a, c in self.q[eng_name]:
            if kind == "wait":
                eng_obj.wait_ge(sems[a], c)
            else:
                inst = a(eng_obj)
                if c is not None:
                    inst.then_inc(sems[c], 16 if c not in self.ENGINES else 1)


def _rotary_tables():
    n = D // 4
    ang = (1.0 / 1024.0) ** np.linspace(0.0, 1.0, n)
    ang = np.concatenate([ang, np.zeros(D // 2 - n)])
    th = np.arange(T, dtype=np.float64)[:, None] * ang[None, :]
    cos8 = np.tile(np.cos(th), (1, H)).astype(np.float32).astype(bfd)
    sin8 = np.tile(np.sin(th), (1, H)).astype(np.float32).astype(bfd)
    return cos8, sin8


def _static_masks():
    kj = np.arange(128)[:, None]
    v = np.arange(1024)[None, :] - 384
    msm = ((v - kj >= 0) & (v - kj <= WIN - 1)).astype(np.float32).astype(bfd)
    cbm = ((v - kj >= 0) & ((v // BS) == (kj // BS))).astype(np.float32).astype(bfd)
    cmm = (v - kj >= 0).astype(np.float32).astype(bfd)
    u = np.arange(256)[None, :]
    tl = np.arange(128)[:, None]
    cmaster = np.where(16 * (u - 120) <= tl, 0.0, NEG).astype(np.float32).astype(bfd)
    ef = np.zeros((NT, 128, 128), np.float32)
    for kt in range(NT):
        k = np.arange(128)
        ef[kt, 8 * kt + k // BS, k] = 1.0
    return msm, cbm, cmm, cmaster, ef.astype(f8d)


def host_prepare(inp):
    f32 = np.float32
    g = lambda k: np.ascontiguousarray(np.asarray(inp[k], f32))
    x = g("L0_attn_x")
    qkv = g("L0_attn_qkv_w")
    lam = g("L0_attn_lambdas")
    cpj = g("L0_attn_c_proj_w")
    kfc = g("L0_attn_k_fc_w")
    kpj = g("L0_attn_k_pj_w")
    vfc = g("L0_attn_v_fc_w")
    vpj = g("L0_attn_v_pj_w")
    mem = g("L0_attn_mem_kv")
    kpos = g("L0_attn_k_pos")
    vpos = g("L0_attn_v_pos")
    sw = g("L0_attn_strat_w")
    sb = g("L0_attn_strat_b")
    wr = g("L0_ffn_W_r")
    wk = g("L0_ffn_W_k")
    wv = g("L0_ffn_W_v")
    mur = g("L0_ffn_mu_r")
    muk = g("L0_ffn_mu_k")

    C = np.ascontiguousarray
    shared = {
        "QKWQ": C(qkv[0].T.reshape(8, 128, 1024)),
        "QKWK": C(qkv[1].T.reshape(8, 128, 1024)),
        "QWV": C(qkv[2].T.reshape(8, 128, 8, 128).transpose(2, 1, 0, 3).reshape(8, 128, 1024)),
        "STRATW": C(sw.T.reshape(8, 128, 24)),
        "B24": C(np.tile(sb[None, :], (128, 1))),
        "LAM0": np.broadcast_to(lam[0:1], (128, 1)).copy(),
        "FCK": C(kfc.reshape(64, 128, 16, 128).transpose(0, 3, 2, 1).reshape(64, 128, 2048)),
        "FCV": C(vfc.reshape(64, 128, 16, 128).transpose(0, 3, 2, 1).reshape(64, 128, 2048)),
        "PJK": C(kpj.T.reshape(64, 128, 128).transpose(1, 0, 2).reshape(128, 8192)),
        "PJV": C(vpj.T.reshape(64, 128, 128).transpose(1, 0, 2).reshape(128, 8192)),
        "MEMKT": C(mem[0].reshape(8, 128).T),
        "MEMV": C(mem[1].reshape(1, 1024)),
        "KPOST": C(kpos.transpose(2, 0, 1).reshape(128, 128)),
        "VPOST": C(vpos.transpose(2, 0, 1).reshape(128, 128)),
        "CPW": C(cpj.T.reshape(8, 128, 1024)),
        "WRT": C(wr.T.reshape(8, 128, 1024)),
        "WKT": C(wk.T.reshape(8, 128, 1024)),
        "WVT": C(wv.T.reshape(8, 128, 1024)),
        "MUR": C(mur.reshape(8, 128).T),
        "MUK": C(muk.reshape(8, 128).T),
        "IDB": np.eye(128, dtype=np.float32).astype(bfd),
        "ONESB": np.ones((128, 1), np.float32).astype(bfd),
        "ONESR": np.ones((1, 128), np.float32).astype(bfd),
    }
    cos8, sin8 = _rotary_tables()
    shared["COS8"] = C(cos8.reshape(NT, 128, 512))
    shared["SIN8"] = C(sin8.reshape(NT, 128, 512))
    msm, cbm, cmm, cmaster, ef = _static_masks()
    shared["MSM"] = msm
    shared["CBM"] = cbm
    shared["CM"] = cmm
    shared["CMASTER"] = cmaster
    shared["EFULL"] = C(ef.transpose(1, 0, 2).reshape(128, NT * 128))
    in_maps = []
    for c in range(8):
        m = dict(shared)
        m["X"] = C(x[c])
        in_maps.append(m)
    return in_maps


def build_program(dbg=(), phases=99):
    nc = bass.Bass("TRN2", target_bir_lowering=False, debug=False)
    dt = nc.dram_tensor
    X = dt("X", [T, DM], F32, kind="ExternalInput").ap()
    QKWQ = dt("QKWQ", [8, 128, 1024], F32, kind="ExternalInput").ap()
    QKWK = dt("QKWK", [8, 128, 1024], F32, kind="ExternalInput").ap()
    QWV = dt("QWV", [8, 128, 1024], F32, kind="ExternalInput").ap()
    STRATW = dt("STRATW", [8, 128, 24], F32, kind="ExternalInput").ap()
    B24 = dt("B24", [128, 24], F32, kind="ExternalInput").ap()
    LAM0 = dt("LAM0", [128, 1], F32, kind="ExternalInput").ap()
    FCK = dt("FCK", [64, 128, 2048], F32, kind="ExternalInput").ap()
    FCV = dt("FCV", [64, 128, 2048], F32, kind="ExternalInput").ap()
    PJK = dt("PJK", [128, 8192], F32, kind="ExternalInput").ap()
    PJV = dt("PJV", [128, 8192], F32, kind="ExternalInput").ap()
    MEMKT = dt("MEMKT", [128, 8], F32, kind="ExternalInput").ap()
    MEMV = dt("MEMV", [1, 1024], F32, kind="ExternalInput").ap()
    KPOST = dt("KPOST", [128, 128], F32, kind="ExternalInput").ap()
    VPOST = dt("VPOST", [128, 128], F32, kind="ExternalInput").ap()
    CPW = dt("CPW", [8, 128, 1024], F32, kind="ExternalInput").ap()
    WRT = dt("WRT", [8, 128, 1024], F32, kind="ExternalInput").ap()
    WKT = dt("WKT", [8, 128, 1024], F32, kind="ExternalInput").ap()
    WVT = dt("WVT", [8, 128, 1024], F32, kind="ExternalInput").ap()
    MUR = dt("MUR", [128, 8], F32, kind="ExternalInput").ap()
    MUK = dt("MUK", [128, 8], F32, kind="ExternalInput").ap()
    IDB = dt("IDB", [128, 128], BF16, kind="ExternalInput").ap()
    ONESBd = dt("ONESB", [128, 1], BF16, kind="ExternalInput").ap()
    ONESRd = dt("ONESR", [1, 128], BF16, kind="ExternalInput").ap()
    COS8 = dt("COS8", [NT, 128, 512], BF16, kind="ExternalInput").ap()
    SIN8 = dt("SIN8", [NT, 128, 512], BF16, kind="ExternalInput").ap()
    MSMd = dt("MSM", [128, 1024], BF16, kind="ExternalInput").ap()
    CBMd = dt("CBM", [128, 1024], BF16, kind="ExternalInput").ap()
    CMd = dt("CM", [128, 1024], BF16, kind="ExternalInput").ap()
    CMASTERd = dt("CMASTER", [128, 256], BF16, kind="ExternalInput").ap()
    EFULLd = dt("EFULL", [128, NT * 128], FP8, kind="ExternalInput").ap()

    XOUT = dt("XOUT", [T, DM], F32, kind="ExternalOutput").ap()
    X1SPILL = dt("X1SPILL", [T, DM], F32, kind="Internal").ap()
    dbg_t = {}
    for name, shape, dty in [
        ("QFM", [128, 16384], BF16), ("KFM", [128, 16384], BF16),
        ("VFM", [128, 16384], BF16), ("VTOK", [128, 16384], BF16),
        ("STRAT", [128, NT * 24], F32), ("CKFM", [128, 1024], BF16),
        ("CVTOK", [128, 1024], BF16), ("CVMEM", [1, 1024], BF16),
        ("KCALL", [128, 16384], BF16), ("SELT", [NG, 128, 512], BF16),
        ("ATT", [NG, 128, 4096], BF16), ("X1", [T, DM], F32),
        ("XT0", [128, 1024], F32), ("LNTOK0", [128, 1024], BF16),
        ("YFM", [128, 16392], BF16), ("KAT", [128, 16384], BF16),
        ("RSIG15", [128, 1024], BF16), ("X1LD15", [128, 1024], F32),
        ("KCALLK", [128, 16384], BF16),
        ("LNT", [128, 16384], BF16), ("QTOK0", [128, 1024], BF16),
        ("ROT0", [128, 2048], BF16),
    ]:
        if name in dbg:
            dbg_t[name] = dt("DBG_" + name, shape, dty, kind="ExternalOutput").ap()

    b = Build()
    st = ExitStack()
    en = st.enter_context
    with st:
        R_qy = en(nc.sbuf_tensor("R_qy", [128, 16400], BF16))
        R_kka = en(nc.sbuf_tensor("R_kka", [128, 16384], BF16))
        R_w = en(nc.sbuf_tensor("R_w", [128, 16384], BF16))
        R_vs = en(nc.sbuf_tensor("R_vs", [128, 16384], BF16))
        R_kc = en(nc.sbuf_tensor("R_kc", [128, 16384], BF16))
        R_ps = en(nc.sbuf_tensor("R_ps", [128, 9728], BF16))
        R_ws = en(nc.sbuf_tensor("R_ws", [128, 6144], BF16))

        idb = en(nc.sbuf_tensor("idb", [128, 128], BF16))
        onesb = en(nc.sbuf_tensor("onesb", [128, 1], BF16))
        onesr = en(nc.sbuf_tensor("onesr", [1, 128], BF16))
        cmm = en(nc.sbuf_tensor("cmm", [128, 1024], BF16))
        cmaster = en(nc.sbuf_tensor("cmaster", [128, 256], BF16))
        efull = en(nc.sbuf_tensor("efull", [128, NT * 128], FP8))
        memkt = en(nc.sbuf_tensor("memkt", [128, 8], F32))
        memkt_bf = en(nc.sbuf_tensor("memkt_bf", [128, 8], BF16))
        lam0 = en(nc.sbuf_tensor("lam0", [128, 1], F32))
        b24x = en(nc.sbuf_tensor("b24x", [128, 24], F32))
        b24b = en(nc.sbuf_tensor("b24b", [1, 48], BF16))
        mur_sb = en(nc.sbuf_tensor("mur_sb", [128, 8], F32))
        muk_sb = en(nc.sbuf_tensor("muk_sb", [128, 8], F32))
        omur_sb = en(nc.sbuf_tensor("omur_sb", [128, 8], F32))
        omuk_sb = en(nc.sbuf_tensor("omuk_sb", [128, 8], F32))
        stratw_sb = en(nc.sbuf_tensor("stratw_sb", [128, 8 * 24], BF16))
        strat_sig = en(nc.sbuf_tensor("strat_sig", [128, NT * 24], F32))
        ck_fm = en(nc.sbuf_tensor("ck_fm", [128, 1024], BF16))
        cv_tok = en(nc.sbuf_tensor("cv_tok", [128, 1024], BF16))
        stat_scr = [en(nc.sbuf_tensor(f"stat{i}", [128, 8], F32)) for i in range(3)]
        rr_tab = [en(nc.sbuf_tensor(f"rr_tab{i}", [128, 16], F32)) for i in range(2)]
        pt_sb = [en(nc.sbuf_tensor(f"pt{i}", [128, 128], BF16)) for i in range(4)]
        csim_sb = [en(nc.sbuf_tensor(f"csim{i}", [128, 129], F32)) for i in range(4)]
        pn_sb = [en(nc.sbuf_tensor(f"pn{i}", [128, 129], BF16)) for i in range(4)]
        memsc = en(nc.sbuf_tensor("memsc", [128, 8], F32))
        small = [en(nc.sbuf_tensor(f"small{i}", [128, 8], F32)) for i in range(4)]
        qtab = en(nc.sbuf_tensor("qtab", [128, 32], F32))
        s24_scr = en(nc.sbuf_tensor("s24_scr", [128, 24], F32))
        epsT = en(nc.sbuf_tensor("epsT", [128, 4], F32))

        ps_q = en(nc.psum_tensor("ps_q", [128, 1024], F32))
        ps_k = en(nc.psum_tensor("ps_k", [128, 1024], F32))
        ps_v = en(nc.psum_tensor("ps_v", [128, 1024], F32))
        ps_misc = en(nc.psum_tensor("ps_misc", [128, 512], F32))
        ps_trb = en(nc.psum_tensor("ps_trb", [128, 768], BF16))
        ps_s24 = ps_misc[:, 0:24]
        ps_sums = ps_misc[:, 32:40]
        ps_tr = ps_trb[:, 0:512]
        ps_ptr = ps_trb[:, 512:768]

        q_fm = R_qy[:, 0:16384]
        y_fm = R_qy[:, 0:16392]
        k_fm = R_kka[:, 0:16384]
        kaT = R_kka[:, 0:16384]
        qkw_sb = R_w[:, 0:16384]
        cpw_sb = R_w[:, 0:8192]
        wrt_sb = R_w[:, 0:8192]
        wkt_sb = R_w[:, 8192:16384]
        v_fm = R_vs[:, 0:16384]
        v_tok = R_vs[:, 0:16384]
        x_tile = [R_vs[:, i * 2048:(i + 1) * 2048].bitcast(F32) for i in range(3)]
        lnT = R_kc[:, 0:16384]
        kcall = R_kc[:, 0:16384]
        m_f32 = R_kc[:, 0:4096].bitcast(F32)
        sel_bf = R_kc[:, 4096:4608]
        mf01 = R_kc[:, 4608:12800]
        selt = R_kc[:, 12800:13312]
        eq_scr = R_kc[:, 13312:14336].bitcast(F32)
        selt250 = R_kc[:, 13312:13824]
        e2c = R_kc[:, 0:1024]
        pf2c = R_kc[:, 1024:2048]
        pslc = R_kc[:, 2048:3072]
        mfd_scr = R_kc[:, 14336:14848]
        wvt6a = R_kc[:, 0:8192]
        qt_scr2 = [R_ps[:, 0:1024], R_ws[:, 0:1024]]
        kt_scr2 = [R_ps[:, 1024:2048], R_ws[:, 1024:2048]]
        ln_tok = [R_ps[:, 2048 + i * 1024: 3072 + i * 1024] for i in range(3)]
        cos_s = [R_ps[:, 5120 + i * 512: 5632 + i * 512] for i in range(2)]
        sin_s = [R_ps[:, 6144 + i * 512: 6656 + i * 512] for i in range(2)]
        rot_q = [R_ps[:, 7168:8192], R_ws[:, 2048:3072]]
        rot_k = [R_ps[:, 8192:9216], R_ws[:, 3072:4096]]
        rot_tmp = R_ps[:, 9216:9728]
        disc = R_vs[:, 6144:7168]
        sq_scr = R_ps[:, 8704:9728]
        hid_f32 = [R_ps[:, i * 2048:(i + 1) * 2048].bitcast(F32) for i in range(2)]
        hid_bf = [R_ps[:, 4096 + i * 1024: 5120 + i * 1024] for i in range(2)]
        cv_fm_scr = R_ps[:, 6144:7168]
        kpost = R_ps[:, 7168:7424].bitcast(F32)
        vpost = R_ps[:, 7424:7680].bitcast(F32)
        spill_ev = R_ps[:, 1024:3072].bitcast(F32)
        pv_ev = R_ps[:, 1024:2048]
        sums_ev = R_ps[:, 2048:2064].bitcast(F32)
        psl2 = [R_ps[:, 0:1024], R_ps[:, 8192:9216]]
        att_acc = R_ps[:, 3072:7168]
        y_tok2 = [R_ps[:, 0:1024], R_ps[:, 1024:2048]]
        qwv_s = [R_ws[:, i * 1024:(i + 1) * 1024] for i in range(2)]
        wfc_s = [R_ws[:, i * 2048:(i + 1) * 2048] for i in range(2)]
        pj_s = [R_ws[:, 4096 + i * 1024: 5120 + i * 1024] for i in range(2)]
        e2 = [R_ws[:, i * 1024:(i + 1) * 1024] for i in range(2)]
        pf2 = [R_ws[:, 2048 + i * 1024: 3072 + i * 1024] for i in range(2)]
        attT = R_ws[:, 0:1024]
        memv_bf = R_ps[0:1, 8192:9216]
        memvx = R_ps[:, 7168:8192]
        msm = R_ws[:, 4096:5120]
        cbm = R_ws[:, 5120:6144]
        wvt6b = R_ws[:, 0:6144]
        r_sig = R_vs[:, 0:1024]
        ka_tok = R_vs[:, 1024:2048]
        k2relu = R_vs[:, 2048:4096].bitcast(F32)
        xr_sc2 = [R_vs[:, 4096:5120], R_kc[:, 8192:9216]]
        xk_sc2 = [R_vs[:, 5120:6144], R_kc[:, 9216:10240]]
        x1_ld = [R_vs[:, 6144:8192].bitcast(F32), R_vs[:, 8192:10240].bitcast(F32),
                 R_kc[:, 10240:12288].bitcast(F32)]
        x6_ld = [R_vs[:, 10240:12288].bitcast(F32), R_vs[:, 12288:14336].bitcast(F32),
                 R_kc[:, 12288:14336].bitcast(F32)]
        mix_tmp = R_ps[:, 3072:4096]
        xout_sb = R_vs[:, 14336:16384].bitcast(F32)

        def ld(queue, dst, src, sem):
            return b.dma(queue, lambda e, d=dst, s=src: e.dma_start(out=d, in_=s), sem)

        t_const = None
        for dst, src in [(idb[:], IDB), (onesb[:], ONESBd), (onesr[:], ONESRd),
                         (cmm[:], CMd),
                         (cmaster[:], CMASTERd), (efull[:], EFULLd),
                         (memkt[:], MEMKT), (lam0[:], LAM0), (b24x[:], B24),
                         (mur_sb[:], MUR), (muk_sb[:], MUK)]:
            t_const = ld("sp", dst, src, "dconst")
        t_mkb = ld("pool", memkt_bf[:], MEMKT, "dconstp")
        t_stw = ld("pool", stratw_sb[:].rearrange("p (c n) -> p c n", n=24),
                   STRATW.rearrange("c p n -> p c n"), "dconstp")
        qk_view = qkw_sb.rearrange("p (c s co) -> p c s co", s=2, co=1024)
        ld("pool", qk_view[:, :, 0, :], QKWQ.rearrange("c p co -> p c co"), "dconstp")
        t_qkw = ld("pool", qk_view[:, :, 1, :], QKWK.rearrange("c p co -> p c co"), "dconstp")
        b.op("dve", lambda e: e.memset(epsT[:, 0:1], 1e-5))
        b.op("dve", lambda e: e.memset(epsT[:, 1:2], 1e-6))
        b.op("dve", lambda e: e.memset(epsT[:, 2:3], 10.48576))
        t_eps = b.op_inc("dve", lambda e: e.memset(epsT[:, 3:4], 6.931471805599453))
        b.wait("act", t_eps)
        b.wait("dve", t_const)
        b.op("dve", lambda e: e.tensor_scalar(out=omur_sb[:], in0=mur_sb[:], scalar1=1.0,
                                              scalar2=-1.0, op0=ALU.subtract, op1=ALU.mult))
        b.op("dve", lambda e: e.tensor_scalar(out=omuk_sb[:], in0=muk_sb[:], scalar1=1.0,
                                              scalar2=-1.0, op0=ALU.subtract, op1=ALU.mult))
        t_bh = b.op_inc("dve", lambda e: e.tensor_copy(out=b24b[:, 0:24], in_=b24x[0:1, :]))
        b.wait("dve", t_bh)
        t_bl = b.op_inc("dve", lambda e: e.tensor_sub(
            s24_scr[0:1, :], b24x[0:1, :], b24b[:, 0:24]))
        b.wait("dve", t_bl)
        t_b24 = b.op_inc("dve", lambda e: e.tensor_copy(
            out=b24b[:, 24:48], in_=s24_scr[0:1, :]))
        b.wait("pe", t_b24)

        t_x = [None] * (2 * NT)
        t_cs = [None] * NT
        t_x[0] = ld("sp", x_tile[0][:, 0:1024], X[0:128, :], "dx0")
        t_x[1] = ld("sp", x_tile[1][:, 0:1024], X[128:256, :], "dx1")
        t_x[2] = ld("sp", x_tile[2][:, 0:1024], X[256:384, :], "dx2")
        for i in range(2):
            ld("sp", cos_s[i], COS8[i], f"dcos{i}")
            t_cs[i] = ld("sp", sin_s[i], SIN8[i], f"dcos{i}")

        lnTe = [None] * NT
        t_ln_tok = [None] * NT
        ev_done = [None] * NT
        qk_mm_done = [None] * NT
        strat_ev = [None] * NT
        tr_ev_done = [None] * NT
        rot_done = [None] * NT
        OD = 1.0 / DM

        b.wait("pe", t_qkw)
        b.wait("pe", t_const)

        psvb = ps_v[:].bitcast(BF16)

        def emit_qk_transposes(i):
            b.wait("pe", rot_done[i])
            if i >= 1:
                b.wait("pe", tr_ev_done[i - 1])
            tqt = None
            for s, rbuf in ((0, rot_q[i % 2]), (1, rot_k[i % 2])):
                for h in range(H):
                    slot = s * 8 + h
                    out = ps_tr[:, slot * 128:(slot + 1) * 128] if slot < 4 else \
                        psvb[:, (slot - 4) * 128:(slot - 3) * 128]
                    fn = lambda e, rbuf=rbuf, h=h, out=out: e.transpose(
                        out=out, in_=rbuf[:, h * 128:(h + 1) * 128], identity=idb[:])
                    tqt = b.op_inc("pe", fn) if slot == 15 else b.op("pe", fn)
            b.wait("act", tqt)
            qv = q_fm.rearrange("p (h t) -> p h t", t=2048)
            kv = k_fm.rearrange("p (h t) -> p h t", t=2048)
            b.op("act", lambda e, i=i, qv=qv: e.copy(
                out=qv[:, 0:4, i * 128:(i + 1) * 128],
                in_=ps_tr[:].rearrange("p (c t) -> p c t", t=128)))
            b.op("act", lambda e, i=i, qv=qv: e.copy(
                out=qv[:, 4:8, i * 128:(i + 1) * 128],
                in_=psvb[:, 0:512].rearrange("p (c t) -> p c t", t=128)))
            tev = b.op_inc("act", lambda e, i=i, kv=kv: e.copy(
                out=kv[:, 0:8, i * 128:(i + 1) * 128],
                in_=psvb[:, 512:1536].rearrange("p (c t) -> p c t", t=128)))
            tr_ev_done[i] = tev

        RMSSC = 0.08838834764831845
        t_f2_tab = [None] * NT
        ln_head_u = [None] * NT

        def emit_ln_head(i):
            b3 = i % 3
            st = stat_scr[b3]
            b.wait("act", t_x[i])
            t_ssq = b.op_inc("act", lambda e, b3=b3, st=st: e.activation(
                out=disc, in_=x_tile[b3][:, 0:1024], func=AF.Square, scale=32.0,
                accum_out=st[:, 1:2]))
            b.wait("dve", t_x[i])
            t_sum = b.op_inc("dve", lambda e, b3=b3, st=st: e.reduce_sum(
                out=st[:, 0:1], in_=x_tile[b3][:, 0:1024], axis=AX.X))
            b.wait("dve", t_sum)
            b.wait("dve", t_ssq)
            b.op("dve", lambda e, st=st: e.tensor_scalar(
                out=st[:, 2:3], in0=st[:, 0:1], scalar1=OD, scalar2=None, op0=ALU.mult))
            ln_head_u[i] = b.op_inc("dve", lambda e, st=st: e.scalar_tensor_tensor(
                out=st[:, 4:5], in0=st[:, 0:1], scalar=st[:, 0:1], in1=st[:, 1:2],
                op0=ALU.mult, op1=ALU.subtract))

        def emit_ln_tail(i):
            b3 = i % 3
            st = stat_scr[b3]
            b.wait("act", ln_head_u[i])
            t_ln5 = b.op_inc("act", lambda e, st=st: e.activation(
                out=st[:, 5:6], in_=st[:, 4:5], func=AF.Ln, scale=-1.0,
                bias=epsT[:, 2:3]))
            b.wait("act", t_ln5)
            t_rstd = b.op_inc("act", lambda e, st=st: e.activation(
                out=st[:, 6:7], in_=st[:, 5:6], func=AF.Exp, scale=-0.5,
                bias=epsT[:, 3:4]))
            b.wait("dve", t_rstd)
            t_ln_tok[i] = b.op_inc("dve", lambda e, b3=b3, st=st: e.scalar_tensor_tensor(
                out=ln_tok[b3], in0=x_tile[b3][:, 0:1024], scalar=st[:, 2:3],
                in1=st[:, 6:7].broadcast_to([128, 1024]),
                op0=ALU.subtract, op1=ALU.mult))

        def emit_rotary(j):
            b2r = j % 2
            qscr, kscr = qt_scr2[b2r], kt_scr2[b2r]
            b.wait("dve", t_cs[j])
            b.wait("dve", t_f2_tab[j])
            b.wait("dve", ev_done[j])
            if j >= 2:
                b.wait("dve", tr_ev_done[j - 2])
            tile_rot = None
            for s, scr, rbuf in ((0, qscr, rot_q[b2r]), (1, kscr, rot_k[b2r])):
                v3 = lambda ap: ap.rearrange("p (h a j) -> p h a j", a=2, j=64)
                sv = v3(scr)
                rv = v3(rbuf)
                cs = cos_s[b2r].rearrange("p (h j) -> p h j", j=64)
                sn = sin_s[b2r].rearrange("p (h j) -> p h j", j=64)
                tm = rot_tmp.rearrange("p (h j) -> p h j", j=64)
                b.op("dve", lambda e, sv=sv, rv=rv, cs=cs: e.tensor_tensor(
                    out=rv[:, :, 0, :], in0=sv[:, :, 0, :], in1=cs, op=ALU.mult))
                b.op("dve", lambda e, sv=sv, rv=rv, sn=sn: e.tensor_tensor(
                    out=rv[:, :, 1, :], in0=sv[:, :, 0, :], in1=sn, op=ALU.mult))
                b.op("dve", lambda e, sv=sv, tm=tm, sn=sn: e.tensor_tensor(
                    out=tm, in0=sv[:, :, 1, :], in1=sn, op=ALU.mult))
                b.op("dve", lambda e, rv=rv, tm=tm: e.tensor_tensor(
                    out=rv[:, :, 0, :], in0=rv[:, :, 0, :], in1=tm, op=ALU.add))
                b.op("dve", lambda e, sv=sv, tm=tm, cs=cs: e.tensor_tensor(
                    out=tm, in0=sv[:, :, 1, :], in1=cs, op=ALU.mult))
                b.op("dve", lambda e, rv=rv, tm=tm: e.tensor_sub(
                    rv[:, :, 1, :], tm, rv[:, :, 1, :]))
                for h in range(H):
                    fn = lambda e, rbuf=rbuf, h=h, s=s, b2r=b2r: e.tensor_scalar(
                        out=rbuf[:, h * 128:(h + 1) * 128],
                        in0=rbuf[:, h * 128:(h + 1) * 128],
                        scalar1=rr_tab[b2r][:, s * 8 + h: s * 8 + h + 1], scalar2=None,
                        op0=ALU.mult)
                    if s == 1 and h == H - 1:
                        tile_rot = b.op_inc("dve", fn)
                    else:
                        b.op("dve", fn)
            rot_done[j] = tile_rot
            if j == 0 and "QTOK0" in dbg_t:
                b.wait("sp", ev_done[0])
                ld("sp", dbg_t["QTOK0"], qt_scr2[0], "ddbg")
            if j == 0 and "ROT0" in dbg_t:
                b.wait("sp", tile_rot)
                ld("sp", dbg_t["ROT0"][:, 0:1024], rot_q[0], "ddbg")
                ld("sp", dbg_t["ROT0"][:, 1024:2048], rot_k[0], "ddbg")

        emit_ln_head(0)
        emit_ln_tail(0)
        for i in range(NT):
            b3 = i % 3
            b2 = i % 2
            if i + 1 < NT:
                emit_ln_head(i + 1)
            if i == 0 and "XT0" in dbg_t:
                b.wait("sp", t_ln_tok[0])
                ld("sp", dbg_t["XT0"], x_tile[0][:, 0:1024], "ddbg")
            if i == 0 and "LNTOK0" in dbg_t:
                b.wait("sp", t_ln_tok[0])
                ld("sp", dbg_t["LNTOK0"], ln_tok[0], "ddbg")

            b.wait("pe", t_ln_tok[i])
            if i >= 3:
                b.wait("pe", tr_ev_done[i - 3])
            tgt1 = tgt2 = None
            for ci in range(8):
                out = ps_tr[:, ci * 128:(ci + 1) * 128] if ci < 4 else \
                    psvb[:, (ci - 4) * 128:(ci - 3) * 128]
                fn = lambda e, b3=b3, ci=ci, out=out: e.transpose(
                    out=out, in_=ln_tok[b3][:, ci * 128:(ci + 1) * 128], identity=idb[:])
                if ci == 3:
                    tgt1 = b.op_inc("pe", fn)
                elif ci == 7:
                    tgt2 = b.op_inc("pe", fn)
                else:
                    b.op("pe", fn)
            lv = lnT.rearrange("p (c t) -> p c t", t=2048)
            b.wait("dve", tgt1)
            eg1 = b.op_inc("dve", lambda e, i=i, lv=lv: e.tensor_copy(
                out=lv[:, 0:4, i * 128:(i + 1) * 128],
                in_=ps_tr[:].rearrange("p (c t) -> p c t", t=128)))
            b.wait("dve", tgt2)
            eg2 = b.op_inc("dve", lambda e, i=i, lv=lv: e.tensor_copy(
                out=lv[:, 4:8, i * 128:(i + 1) * 128],
                in_=psvb[:, 0:512].rearrange("p (c t) -> p c t", t=128)))
            lnTe[i] = eg2
            if i + 1 < NT:
                emit_ln_tail(i + 1)

            if i >= 1:
                b.wait("pe", ev_done[i - 1])
                b.wait("pe", strat_ev[i - 1])
            b.op("pe", lambda e: e.matmul(
                out=ps_s24, lhsT=onesr[:], rhs=b24b[:, 0:24],
                start=True, stop=False, skip_group_check=True))
            b.op("pe", lambda e: e.matmul(
                out=ps_s24, lhsT=onesr[:], rhs=b24b[:, 24:48],
                start=False, stop=False, skip_group_check=True))
            tqk = None
            for ci in range(8):
                if ci == 0:
                    b.wait("pe", eg1)
                if ci == 4:
                    b.wait("pe", eg2)
                lhs = lnT[:, ci * 2048 + i * 128: ci * 2048 + (i + 1) * 128]
                for hh in range(2):
                    b.op("pe", lambda e, lhs=lhs, ci=ci, hh=hh: e.matmul(
                        out=ps_q[:, hh * 512:(hh + 1) * 512], lhsT=lhs,
                        rhs=qkw_sb[:, ci * 2048 + hh * 512: ci * 2048 + (hh + 1) * 512],
                        start=(ci == 0), stop=(ci == 7), skip_group_check=True))
                    b.op("pe", lambda e, lhs=lhs, ci=ci, hh=hh: e.matmul(
                        out=ps_k[:, hh * 512:(hh + 1) * 512], lhsT=lhs,
                        rhs=qkw_sb[:, ci * 2048 + 1024 + hh * 512:
                                   ci * 2048 + 1024 + (hh + 1) * 512],
                        start=(ci == 0), stop=(ci == 7), skip_group_check=True))
                fn = lambda e, lhs=lhs, ci=ci: e.matmul(
                    out=ps_s24, lhsT=lhs, rhs=stratw_sb[:, ci * 24:(ci + 1) * 24],
                    start=False, stop=(ci == 7), skip_group_check=True)
                tqk = b.op_inc("pe", fn) if ci == 7 else b.op("pe", fn)
            qk_mm_done[i] = tqk

            b.wait("act", tqk)
            if i >= 2:
                b.wait("act", rot_done[i - 2])
            qscr, kscr = qt_scr2[b2], kt_scr2[b2]
            b.op("act", lambda e, qscr=qscr: e.copy(out=qscr, in_=ps_q[:]))
            tev2 = b.op_inc("act", lambda e, kscr=kscr: e.copy(out=kscr, in_=ps_k[:]))
            ev_done[i] = tev2
            strat_ev[i] = b.op_inc("act", lambda e, i=i: e.activation(
                out=strat_sig[:, i * 24:(i + 1) * 24], in_=ps_s24, func=AF.Sigmoid))
            trr = None
            for s, scr in ((0, qscr), (1, kscr)):
                for h in range(H):
                    fn = lambda e, scr=scr, s=s, h=h, b2=b2: e.activation(
                        out=disc[:, 0:128], in_=scr[:, h * 128:(h + 1) * 128],
                        func=AF.Square, scale=RMSSC,
                        accum_out=rr_tab[b2][:, s * 8 + h: s * 8 + h + 1])
                    trr = b.op_inc("act", fn) if (s, h) == (1, 7) else b.op("act", fn)

            if i >= 2:
                emit_qk_transposes(i - 2)

            b.wait("act", trr)
            t_fln = b.op_inc("act", lambda e, b2=b2: e.activation(
                out=rr_tab[b2][:], in_=rr_tab[b2][:], func=AF.Ln, bias=epsT[:, 1:2]))
            b.wait("act", t_fln)
            t_f2_tab[i] = b.op_inc("act", lambda e, b2=b2: e.activation(
                out=rr_tab[b2][:], in_=rr_tab[b2][:], func=AF.Exp, scale=-0.5))

            if i >= 1:
                emit_rotary(i - 1)
            if i + 3 < NT:
                b.wait("sp", t_ln_tok[i])
                t_x[i + 3] = ld("sp", x_tile[b3][:, 0:1024],
                                X[(i + 3) * 128:(i + 4) * 128, :], f"dx{b3}")
            if i >= 1 and i + 1 < NT:
                b2n = (i + 1) % 2
                b.wait("sp", rot_done[i - 1])
                ld("sp", cos_s[b2n], COS8[i + 1], f"dcos{b2n}")
                t_cs[i + 1] = ld("sp", sin_s[b2n], SIN8[i + 1], f"dcos{b2n}")

        emit_rotary(NT - 1)
        emit_qk_transposes(NT - 2)
        emit_qk_transposes(NT - 1)

        t_qwv = [None] * 8
        b.wait("pool", rot_done[NT - 1])
        for idx in range(2):
            t_qwv[idx] = ld("pool", qwv_s[idx], QWV[idx], f"dqwv{idx}")
        v_ev = [None] * 8
        b.wait("pe", qk_mm_done[NT - 1])
        b.wait("pe", tr_ev_done[NT - 1])
        b.wait("act", rot_done[NT - 1])
        for co in range(8):
            wb = co % 2
            pbank = co % 2
            b.wait("pe", t_qwv[co])
            if co >= 2:
                b.wait("pe", v_ev[co - 2])
            for tg in range(4):
                tvm = None
                for ci in range(8):
                    fn = lambda e, ci=ci, wb=wb, tg=tg, pbank=pbank: e.matmul(
                        out=ps_v[:, pbank * 512:(pbank + 1) * 512],
                        lhsT=qwv_s[wb][:, ci * 128:(ci + 1) * 128],
                        rhs=lnT[:, ci * 2048 + tg * 512: ci * 2048 + (tg + 1) * 512],
                        start=(ci == 0), stop=(ci == 7), skip_group_check=True)
                    tvm = b.op_inc("pe", fn) if ci == 7 else b.op("pe", fn)
                b.wait("act", tvm)
                tko = b.op_inc("act", lambda e, co=co, tg=tg, pbank=pbank: e.activation(
                    out=v_fm[:, co * 2048 + tg * 512: co * 2048 + (tg + 1) * 512],
                    in_=ps_v[:, pbank * 512:(pbank + 1) * 512], func=AF.Copy,
                    scale=lam0[:, 0:1]))
                if tg < 3:
                    b.wait("pe", tko)
                else:
                    v_ev[co] = tko
            if co + 2 < 8:
                b.wait("pool", v_ev[co])
                t_qwv[co + 2] = ld("pool", qwv_s[wb], QWV[co + 2], f"dqwv{wb}")
        t_v_done = v_ev[7]

        b.wait("pool", rot_done[NT - 1])
        b.wait("pool", tr_ev_done[NT - 1])
        ld("pool", kpost[:, 0:128], KPOST, "dpost")
        t_post = ld("pool", vpost[:, 0:128], VPOST, "dpost")

        if "LNT" in dbg_t:
            b.wait("sp", qk_mm_done[0])
            ld("sp", dbg_t["LNT"], lnT, "ddbg")
        if "QFM" in dbg_t:
            b.wait("sp", tr_ev_done[NT - 1])
            ld("sp", dbg_t["QFM"], q_fm, "ddbg")
            ld("sp", dbg_t["KFM"], k_fm, "ddbg")
        if "VFM" in dbg_t:
            b.wait("sp", t_v_done)
            ld("sp", dbg_t["VFM"], v_fm, "ddbg")
        if "STRAT" in dbg_t:
            b.wait("sp", strat_ev[NT - 1])
            ld("sp", dbg_t["STRAT"], strat_sig[:], "ddbg")
        if phases <= 2:
            _finish(b, nc, en)
            return nc, dbg_t

        b.wait("pool", qk_mm_done[NT - 1])
        t_cpw = ld("pool", cpw_sb.rearrange("p (c n) -> p c n", n=1024),
                    CPW.rearrange("c p n -> p c n"), "dcpw")
        t_wkt = ld("pool", wkt_sb.rearrange("p (c n) -> p c n", n=1024),
                    WKT.rearrange("c p n -> p c n"), "dcpw")

        b.wait("dve", t_v_done)
        b.wait("dve", tr_ev_done[NT - 1])
        b.wait("dve", t_post)
        t_kc_asm = None
        for j in range(16):
            for h in range(H):
                fn = lambda e, j=j, h=h: e.tensor_scalar(
                    out=kcall[:, j * 1024 + h * 128: j * 1024 + (h + 1) * 128],
                    in0=k_fm[:, h * 2048 + j: h * 2048 + 2048: 16],
                    scalar1=kpost[:, h * 16 + j: h * 16 + j + 1], scalar2=None,
                    op0=ALU.add)
                t_kc_asm = b.op_inc("dve", fn) if (j, h) == (15, 7) else b.op("dve", fn)

        if "KCALLK" in dbg_t:
            b.wait("sp", t_kc_asm)
            t_kdump = ld("sp", dbg_t["KCALLK"], kcall, "ddbg")
            b.wait("dve", t_kdump)

        def fc_pass(tag, FCW, PJW, asm_token, out_evac, entry_war=(), entry_pool=()):
            t_w = [None] * 64
            t_pj = [None] * 8
            for tw in entry_pool:
                b.wait("pool", tw)
            for idx in range(2):
                t_w[idx] = ld("pool", wfc_s[idx], FCW[idx], f"dwf{tag}{idx}")
                t_pj[idx] = ld("pool", pj_s[idx], PJW[:, idx * 1024:(idx + 1) * 1024],
                               f"dpj{tag}{idx}")
            relu_done = [None, None]
            square_done = [None, None]
            pj_done = [None, None]
            b.wait("pe", asm_token)
            for tw in entry_war:
                b.wait("pe", tw)

            def emit_pj(c):
                buf = c % 2
                g8 = c // 8
                b.wait("pe", square_done[buf])
                if c % 8 == 0:
                    b.wait("pe", t_pj[g8])
                b.op("pe", lambda e, c=c, buf=buf, g8=g8: e.matmul(
                    out=ps_v[:, 0:512],
                    lhsT=pj_s[g8 % 2][:, (c % 8) * 128:(c % 8 + 1) * 128],
                    rhs=hid_bf[buf][:, 0:512],
                    start=(c == 0), stop=(c == 63), skip_group_check=True))
                tpj = b.op_inc("pe", lambda e, c=c, buf=buf, g8=g8: e.matmul(
                    out=ps_v[:, 512:1024],
                    lhsT=pj_s[g8 % 2][:, (c % 8) * 128:(c % 8 + 1) * 128],
                    rhs=hid_bf[buf][:, 512:1024],
                    start=(c == 0), stop=(c == 63), skip_group_check=True))
                pj_done[buf] = tpj
                if c % 8 == 7 and c + 8 < 64:
                    b.wait("pool", tpj)
                    t_pj[c // 8 + 1] = ld("pool", pj_s[(c // 8 + 1) % 2],
                                          PJW[:, (c // 8 + 1) * 1024:(c // 8 + 2) * 1024],
                                          f"dpj{tag}{(c // 8 + 1) % 2}")
                return tpj

            tpj = None
            for c in range(64):
                buf = c % 2
                hidps = ps_q if buf == 0 else ps_k
                b.wait("pe", t_w[c])
                b.wait("pe", relu_done[buf])
                tfc = None
                for j in range(16):
                    for hh in range(2):
                        fn = lambda e, j=j, hh=hh, buf=buf, hidps=hidps: e.matmul(
                            out=hidps[:, hh * 512:(hh + 1) * 512],
                            lhsT=wfc_s[buf][:, j * 128:(j + 1) * 128],
                            rhs=kcall[:, j * 1024 + hh * 512: j * 1024 + (hh + 1) * 512],
                            start=(j == 0), stop=(j == 15), skip_group_check=True)
                        tfc = b.op_inc("pe", fn) if (j == 15 and hh == 1) else b.op("pe", fn)
                if c + 2 < 64:
                    b.wait("pool", tfc)
                    t_w[c + 2] = ld("pool", wfc_s[buf], FCW[c + 2], f"dwf{tag}{buf}")
                b.wait("dve", tfc)
                b.wait("dve", square_done[buf])
                b.op("dve", lambda e, buf=buf, hidps=hidps: e.tensor_scalar_max(
                    out=hid_f32[buf][:, 0:512], in0=hidps[:, 0:512], scalar1=0.0))
                trl = b.op_inc("dve", lambda e, buf=buf, hidps=hidps: e.tensor_scalar_max(
                    out=hid_f32[buf][:, 512:1024], in0=hidps[:, 512:1024], scalar1=0.0))
                relu_done[buf] = trl
                b.wait("act", trl)
                b.wait("act", pj_done[buf])
                b.op("act", lambda e, buf=buf: e.activation(
                    out=hid_bf[buf][:, 0:512], in_=hid_f32[buf][:, 0:512], func=AF.Square))
                tsq = b.op_inc("act", lambda e, buf=buf: e.activation(
                    out=hid_bf[buf][:, 512:1024], in_=hid_f32[buf][:, 512:1024],
                    func=AF.Square))
                square_done[buf] = tsq
                if c > 0:
                    emit_pj(c - 1)
                if c == 63:
                    tpj = emit_pj(c)
            return out_evac(tpj), relu_done, square_done

        def ck_evac(tpj):
            b.wait("dve", tpj)
            return b.op_inc("dve", lambda e: e.tensor_copy(out=ck_fm[:], in_=ps_v[:]))

        t_ck, _, _ = fc_pass("k", FCK, PJK, t_kc_asm, ck_evac,
                             entry_war=(ev_done[NT - 1], t_v_done),
                             entry_pool=(v_ev[6], v_ev[7], tr_ev_done[NT - 1]))

        b.wait("dve", t_ck)
        t_vc_asm = None
        for j in range(16):
            for h in range(H):
                fn = lambda e, j=j, h=h: e.tensor_scalar(
                    out=kcall[:, j * 1024 + h * 128: j * 1024 + (h + 1) * 128],
                    in0=v_fm[:, h * 2048 + j: h * 2048 + 2048: 16],
                    scalar1=vpost[:, h * 16 + j: h * 16 + j + 1], scalar2=None,
                    op0=ALU.add)
                t_vc_asm = b.op_inc("dve", fn) if (j, h) == (15, 7) else b.op("dve", fn)
        if "KCALL" in dbg_t:
            b.wait("sp", t_vc_asm)
            ld("sp", dbg_t["KCALL"], kcall, "ddbg")

        def cv_evac(tpj):
            b.wait("dve", tpj)
            t_cvf = b.op_inc("dve", lambda e: e.tensor_copy(out=cv_fm_scr, in_=ps_v[:]))
            b.wait("pe", t_cvf)
            tev = None
            for grp in range(2):
                tgt = None
                for cc in range(4):
                    h = grp * 4 + cc
                    fn = lambda e, h=h, cc=cc: e.transpose(
                        out=ps_tr[:, cc * 128:(cc + 1) * 128],
                        in_=cv_fm_scr[:, h * 128:(h + 1) * 128], identity=idb[:])
                    tgt = b.op_inc("pe", fn) if cc == 3 else b.op("pe", fn)
                b.wait("act", tgt)
                tev = b.op_inc("act", lambda e, grp=grp: e.copy(
                    out=cv_tok[:, grp * 512:(grp + 1) * 512], in_=ps_tr[:]))
                b.wait("pe", tev)
            return tev

        t_cv, v_relu_done, v_square_done = fc_pass("v", FCV, PJV, t_vc_asm, cv_evac,
                                                   entry_war=(t_ck,),
                                                   entry_pool=(t_ck,))

        b.wait("pool", t_cv)
        ld("pool", memv_bf, MEMV, "dmsk")
        ld("pool", msm, MSMd, "dmsk")
        t_msk = ld("pool", cbm, CBMd, "dmsk")

        b.wait("pe", t_vc_asm)
        perm = [8 * (c % 16) + c // 16 for c in range(128)]
        seen = [False] * 128
        slot = 0
        t_vtok = None
        for c0 in range(128):
            if seen[c0]:
                continue
            cyc = [c0]
            seen[c0] = True
            nxt = perm[c0]
            while nxt != c0:
                cyc.append(nxt)
                seen[nxt] = True
                nxt = perm[nxt]
            tA = b.op_inc("pe", lambda e, c=cyc[0], slot=slot: e.transpose(
                out=ps_ptr[:, (slot % 2) * 128:((slot % 2) + 1) * 128],
                in_=v_tok[:, c * 128:(c + 1) * 128], identity=idb[:]))
            if len(cyc) == 1:
                b.wait("act", tA)
            slotA = slot
            slot += 1
            for k in range(1, len(cyc)):
                ckp = cyc[k]
                tB = b.op_inc("pe", lambda e, c=ckp, slot=slot: e.transpose(
                    out=ps_ptr[:, (slot % 2) * 128:((slot % 2) + 1) * 128],
                    in_=v_tok[:, c * 128:(c + 1) * 128], identity=idb[:]))
                b.wait("act", tB)
                tevv = b.op_inc("act", lambda e, c=ckp, slotA=slotA: e.copy(
                    out=v_tok[:, c * 128:(c + 1) * 128],
                    in_=ps_ptr[:, (slotA % 2) * 128:((slotA % 2) + 1) * 128]))
                b.wait("pe", tevv)
                slotA = slot
                slot += 1
            tevv = b.op_inc("act", lambda e, c=cyc[0], slotA=slotA: e.copy(
                out=v_tok[:, c * 128:(c + 1) * 128],
                in_=ps_ptr[:, (slotA % 2) * 128:((slotA % 2) + 1) * 128]))
            b.wait("pe", tevv)
            t_vtok = tevv

        if "CKFM" in dbg_t:
            b.wait("sp", t_ck)
            ld("sp", dbg_t["CKFM"], ck_fm[:], "ddbg")
        if "CVTOK" in dbg_t:
            b.wait("sp", t_cv)
            ld("sp", dbg_t["CVTOK"], cv_tok[:], "ddbg")
            b.wait("sp", t_msk)
            ld("sp", dbg_t["CVMEM"], memv_bf, "ddbg")
        if "VTOK" in dbg_t:
            b.wait("sp", t_vtok)
            ld("sp", dbg_t["VTOK"], v_tok, "ddbg")
        if phases <= 3:
            _finish(b, nc, en)
            return nc, dbg_t

        t_spill = [None] * NT
        comp_done = [None, None]
        macc_done = [None] * NT
        csim_ev = [None, None, None, None]
        pn_done = [None, None, None, None]
        ptr_ev = [None, None, None, None]
        fine_combine = None
        cproj_done = None
        att_ev = {}
        exp_sl = [None, None, None, None]
        macc_sl = [None, None, None, None]
        t_rc8p = None

        b.wait("pe", t_msk)
        b.wait("pe", t_cv)
        b.wait("pe", t_vtok)
        b.wait("pe", t_mkb)
        b.wait("pe", v_relu_done[0])
        b.wait("pe", v_relu_done[1])
        for half in range(2):
            tmx1 = b.op_inc("pe", lambda e, half=half: e.matmul(
                out=ps_misc[:, 0:512], lhsT=onesr[:],
                rhs=memv_bf[0:1, half * 512:(half + 1) * 512],
                start=True, stop=True, skip_group_check=True))
            b.wait("act", tmx1)
            tmx2 = b.op_inc("act", lambda e, half=half: e.copy(
                out=memvx[:, half * 512:(half + 1) * 512], in_=ps_misc[:, 0:512]))
            b.wait("pe", tmx2)
        t_memvx = tmx2

        class SelTile:

            def __init__(self, qt):
                self.qt = qt
                self.c = qt % 4
                self.mv = m_f32[:, qt * 128:(qt + 1) * 128]
                self.eqs = eq_scr[:, self.c * 128:(self.c + 1) * 128]
                self.mx = small[self.c][:, 5:6]
                self.it = 0
                self.stage = 0
                self.tok = None
                self.done = None
                self.started = False

            def pump(self):
                if self.done is not None:
                    return False
                if not self.started:
                    b.wait("dve", macc_done[self.qt])
                    self.started = True
                if self.stage == 0:
                    self.tok = b.op_inc("dve", lambda e, s=self: e.reduce_max(
                        out=s.mx, in_=s.mv, axis=AX.X))
                    self.stage = 1
                else:
                    b.wait("dve", self.tok)
                    b.op("dve", lambda e, s=self: e.tensor_scalar(
                        out=s.eqs, in0=s.mv, scalar1=s.mx, scalar2=None,
                        op0=ALU.is_equal))
                    b.op("dve", lambda e, s=self: e.scalar_tensor_tensor(
                        out=s.mv, in0=s.eqs, scalar=BIGSUP, in1=s.mv,
                        op0=ALU.mult, op1=ALU.add))
                    self.stage = 0
                    self.it += 1
                    if self.it == 16:
                        self.done = b.op_inc("dve", lambda e, s=self: e.tensor_scalar(
                            out=sel_bf[:, s.c * 128:(s.c + 1) * 128], in0=s.mv,
                            scalar1=SELTH, scalar2=None, op0=ALU.is_lt))
                return True

        for g in range(NG):
            sel_tiles = []

            def pump_sel(n):
                for _ in range(n):
                    for stl in sel_tiles:
                        if stl.pump():
                            break

            for qi in range(4):
                qt = 4 * g + qi
                tcomp = None
                pendPT = None
                pendPV = None

                def flush_pt(pend):
                    hp, hbp, t_exp_p = pend
                    b.wait("pe", t_exp_p)
                    b.wait("pe", ptr_ev[hbp])
                    tpt = b.op_inc("pe", lambda e, hbp=hbp: e.transpose(
                        out=ps_tr[:, hbp * 128:(hbp + 1) * 128],
                        in_=pn_sb[hbp][:, 1:129], identity=idb[:]))
                    b.wait("act", tpt)
                    tpe = b.op_inc("act", lambda e, hbp=hbp: e.copy(
                        out=pt_sb[hbp][:], in_=ps_tr[:, hbp * 128:(hbp + 1) * 128]))
                    ptr_ev[hbp] = tpe
                    return (hp, hbp, tpe)

                def flush_pv4(pend):
                    hp, hbp, tpe = pend
                    b.wait("pe", tpe)
                    if hp == 0:
                        b.wait("pe", comp_done[0])
                    return b.op_inc("pe", lambda e, hp=hp, hbp=hbp: e.matmul(
                        out=ps_v[:, hp * 128:(hp + 1) * 128], lhsT=pt_sb[hbp][:],
                        rhs=cv_tok[:, hp * 128:(hp + 1) * 128],
                        start=(hp in (0, 4)), stop=(hp in (3, 7)),
                        skip_group_check=True))

                for h in range(H):
                    hb = h % 4
                    cps = ps_q[:, hb * 512: hb * 512 + 129] if hb < 2 else \
                        ps_k[:, (hb - 2) * 512: (hb - 2) * 512 + 129]
                    b.wait("pe", csim_ev[hb])
                    b.op("pe", lambda e, h=h, qt=qt, cps=cps: e.matmul(
                        out=cps[:, 0:1],
                        lhsT=q_fm[:, h * 2048 + qt * 128: h * 2048 + (qt + 1) * 128],
                        rhs=memkt_bf[:, h:h + 1],
                        start=True, stop=True, skip_group_check=True))
                    tmm = b.op_inc("pe", lambda e, h=h, qt=qt, cps=cps: e.matmul(
                        out=cps[:, 1:129],
                        lhsT=q_fm[:, h * 2048 + qt * 128: h * 2048 + (qt + 1) * 128],
                        rhs=ck_fm[:, h * 128:(h + 1) * 128],
                        start=False, stop=True, skip_group_check=True))
                    b.wait("dve", tmm)
                    b.wait("dve", exp_sl[hb])
                    b.wait("dve", macc_sl[hb])
                    tcs = b.op_inc("dve", lambda e, hb=hb, qt=qt, cps=cps: e.scalar_tensor_tensor(
                        out=csim_sb[hb][:], in0=cps, scalar=SCALE,
                        in1=cmaster[:, 120 - 8 * qt: 249 - 8 * qt],
                        op0=ALU.mult, op1=ALU.add))
                    csim_ev[hb] = tcs
                    b.wait("pool", tcs)
                    fn = lambda e, hb=hb, qt=qt: e.tensor_copy(
                        out=m_f32[:, qt * 128:(qt + 1) * 128], in_=csim_sb[hb][:, 1:129])
                    fn2 = lambda e, hb=hb, qt=qt: e.tensor_tensor(
                        out=m_f32[:, qt * 128:(qt + 1) * 128],
                        in0=m_f32[:, qt * 128:(qt + 1) * 128],
                        in1=csim_sb[hb][:, 1:129], op=ALU.add)
                    tma = b.op_inc("pool", fn if h == 0 else fn2)
                    macc_sl[hb] = tma
                    if h == H - 1:
                        macc_done[qt] = tma
                    b.wait("act", tcs)
                    b.wait("act", pn_done[hb])
                    if h == 0:
                        b.wait("act", t_rc8p)
                    t_exp = b.op_inc("act", lambda e, hb=hb, h=h: e.activation(
                        out=pn_sb[hb][:], in_=csim_sb[hb][:], func=AF.Exp,
                        accum_out=qtab[:, h:h + 1]))
                    exp_sl[hb] = t_exp
                    b.wait("dve", t_exp)
                    t_fc = b.op_inc("dve", lambda e, hb=hb, h=h: e.tensor_copy(
                        out=qtab[:, 8 + h: 9 + h], in_=pn_sb[hb][:, 0:1]))
                    pn_done[hb] = t_fc
                    nextPV = flush_pt(pendPT) if pendPT is not None else None
                    if pendPV is not None:
                        tcomp = flush_pv4(pendPV)
                    pendPV = nextPV
                    pendPT = (h, hb, t_exp)
                    pump_sel(4)
                nextPV = flush_pt(pendPT)
                if pendPV is not None:
                    tcomp = flush_pv4(pendPV)
                tcomp = flush_pv4(nextPV)
                pendPT = pendPV = None
                b.wait("dve", t_exp)
                t_rc8 = b.op_inc("dve", lambda e: e.reciprocal(
                    out=qtab[:, 16:24], in_=qtab[:, 0:8]))
                t_rc8p = t_rc8
                b.wait("dve", t_rc8)
                t_sc8 = b.op_inc("dve", lambda e, qt=qt: e.tensor_tensor(
                    out=qtab[:, 24:32], in0=qtab[:, 16:24],
                    in1=strat_sig[:, qt * 24: qt * 24 + 24: 3], op=ALU.mult))
                b.wait("dve", t_sc8)
                t_ms8 = b.op_inc("dve", lambda e: e.tensor_tensor(
                    out=memsc[:, 0:8], in0=qtab[:, 8:16], in1=qtab[:, 24:32],
                    op=ALU.mult))
                b.wait("dve", tcomp)
                if "ATT" in dbg_t:
                    b.wait("dve", att_ev.get(g - 1))
                b.wait("dve", t_ms8)
                for h in range(H):
                    b.op("dve", lambda e, h=h, qi=qi: e.tensor_scalar(
                        out=att_acc[:, qi * 1024 + h * 128: qi * 1024 + (h + 1) * 128],
                        in0=ps_v[:, h * 128:(h + 1) * 128],
                        scalar1=qtab[:, 24 + h: 25 + h], scalar2=None, op0=ALU.mult))
                b.wait("dve", t_memvx)
                tmem = None
                for h in range(H):
                    fn = lambda e, h=h, qi=qi: e.scalar_tensor_tensor(
                        out=att_acc[:, qi * 1024 + h * 128: qi * 1024 + (h + 1) * 128],
                        in0=memvx[:, h * 128:(h + 1) * 128], scalar=memsc[:, h:h + 1],
                        in1=att_acc[:, qi * 1024 + h * 128: qi * 1024 + (h + 1) * 128],
                        op0=ALU.mult, op1=ALU.add)
                    tmem = b.op_inc("dve", fn) if h == H - 1 else b.op("dve", fn)
                comp_done[0] = comp_done[1] = tmem
                sel_tiles.append(SelTile(qt))

            while any(stl.done is None for stl in sel_tiles):
                pump_sel(1)
            for stl in sel_tiles:
                b.wait("pe", stl.done)
            for tk in ptr_ev:
                b.wait("pe", tk)
            tgt = None
            for cc in range(4):
                fn = lambda e, cc=cc: e.transpose(
                    out=ps_tr[:, cc * 128:(cc + 1) * 128],
                    in_=sel_bf[:, cc * 128:(cc + 1) * 128], identity=idb[:])
                tgt = b.op_inc("pe", fn) if cc == 3 else b.op("pe", fn)
            b.wait("act", tgt)
            t_selt = b.op_inc("act", lambda e: e.copy(out=selt, in_=ps_tr[:]))
            b.wait("pe", t_selt)
            if "SELT" in dbg_t:
                b.wait("sp", t_selt)
                att_ev[(g, "selt")] = ld("sp", dbg_t["SELT"][g], selt, "ddbg")
                b.wait("dve", att_ev[(g, "selt")])
                b.wait("pe", att_ev[(g, "selt")])

            kmax = 4 * g + 3
            mf_done = [None] * 16
            b.wait("dve", t_selt)
            t_s250 = b.op_inc("dve", lambda e: e.tensor_scalar(
                out=selt250, in0=selt, scalar1=1.0, scalar2=250000.0,
                op0=ALU.subtract, op1=ALU.mult))
            if g > 0:
                b.wait("pe", fine_combine)
            b.wait("pe", comp_done[0])
            for kt in range((max(0, 4 * g - 1)) if FADD else 0, kmax + 1):
                dp = g * 512 - kt * 128
                tsx = b.op_inc("pe", lambda e, kt=kt: e.matmul(
                    out=ps_v[:, 0:512], lhsT=efull[:, kt * 128:(kt + 1) * 128],
                    rhs=selt, start=True, stop=True, skip_group_check=True))
                if dp >= 128:
                    b.wait("act", tsx)
                    tmf = b.op_inc("act", lambda e, kt=kt: e.copy(
                        out=mf01[:, kt * 512:(kt + 1) * 512], in_=ps_v[:, 0:512]))
                    b.wait("pe", tmf)
                else:
                    b.wait("dve", tsx)
                    b.op("dve", lambda e, dp=dp: e.tensor_tensor(
                        out=mfd_scr, in0=ps_v[:, 0:512],
                        in1=cmm[:, dp + 384: dp + 384 + 512], op=ALU.mult))
                    tmf = b.op_inc("dve", lambda e, kt=kt, dp=dp: e.tensor_tensor(
                        out=mf01[:, kt * 512:(kt + 1) * 512], in0=mfd_scr,
                        in1=cbm[:, dp + 384: dp + 384 + 512], op=ALU.max))
                    b.wait("pe", tmf)
                mf_done[kt] = tmf
            b.wait("pe", t_s250)

            exp_done = [None, None]
            pv_done3 = [None, None, None]
            prev_evac = [None]
            pending = []
            for tk in csim_ev:
                b.wait("pe", tk)
            for stl in sel_tiles:
                b.wait("act", stl.done)
            b.wait("act", t_spill[4 * g - 1] if g > 0 else None)
            npair = (kmax + 1) // 2
            e2x = [e2[0], e2[1], e2c]
            pf2x = [pf2[0], pf2[1], pf2c]
            psl2x = [psl2[0], psl2[1], pslc]
            head_last_pv = [None] * H

            def flush_pv():
                h, p, p3, tE, tPF = pending.pop(0)
                flags = head_flags[h]
                b.wait("pe", tPF if tPF is not None else tE)
                if tPF is not None:
                    b.wait("pe", tE)
                if flags[0]:
                    b.wait("pe", prev_evac[0])
                last = None
                for ktip in range(2):
                    kt = 2 * p + ktip
                    dp = g * 512 - kt * 128
                    fadd = FADD and dp > 128
                    slide = -384 <= dp <= 128
                    src = e2x[p3] if fadd else pf2x[p3]
                    for c in range(4):
                        b.op("pe", lambda e, src=src, ktip=ktip, c=c, kt=kt, h=h,
                             ff=flags[0]: e.matmul(
                            out=ps_k[:, c * 128:(c + 1) * 128],
                            lhsT=src[:, ktip * 512 + c * 128: ktip * 512 + (c + 1) * 128],
                            rhs=v_tok[:, kt * 1024 + h * 128: kt * 1024 + (h + 1) * 128],
                            start=ff, stop=False, skip_group_check=True))
                        last = b.op_inc("pe", lambda e, src=src, ktip=ktip, c=c,
                                        sf=flags[2]: e.matmul(
                            out=ps_sums[:, c:c + 1],
                            lhsT=src[:, ktip * 512 + c * 128: ktip * 512 + (c + 1) * 128],
                            rhs=onesb[:], start=sf, stop=False, skip_group_check=True))
                        if slide:
                            b.op("pe", lambda e, p3=p3, ktip=ktip, c=c, kt=kt, h=h,
                                 sf2=flags[1]: e.matmul(
                                out=ps_k[:, 512 + c * 128: 512 + (c + 1) * 128],
                                lhsT=psl2x[p3][:, ktip * 512 + c * 128:
                                               ktip * 512 + (c + 1) * 128],
                                rhs=v_tok[:, kt * 1024 + h * 128: kt * 1024 + (h + 1) * 128],
                                start=sf2, stop=False, skip_group_check=True))
                            last = b.op_inc("pe", lambda e, p3=p3, ktip=ktip, c=c: e.matmul(
                                out=ps_sums[:, 4 + c: 5 + c],
                                lhsT=psl2x[p3][:, ktip * 512 + c * 128:
                                               ktip * 512 + (c + 1) * 128],
                                rhs=onesb[:], start=False, stop=False,
                                skip_group_check=True))
                            flags[1] = False
                        flags[0] = False
                        flags[2] = False
                pv_done3[p3] = last
                if p == npair - 1:
                    head_last_pv[h] = last
                    fc = emit_head_tail(h)
                    return last, fc
                return last, None

            def emit_head_tail(h):
                b.wait("act", head_last_pv[h])
                b.op("act", lambda e: e.copy(out=pv_ev, in_=ps_k[:]))
                tev = b.op_inc("act", lambda e: e.copy(
                    out=sums_ev[:, 0:8], in_=ps_sums[:]))
                prev_evac[0] = tev
                b.wait("dve", tev)
                b.wait("dve", comp_done[0])
                t_rc = b.op_inc("dve", lambda e: e.reciprocal(
                    out=small[2][:, 0:8], in_=sums_ev[:, 0:8]))
                b.wait("dve", t_rc)
                t_sc = None
                for c in range(4):
                    qt = 4 * g + c
                    b.op("dve", lambda e, c=c, qt=qt, h=h: e.tensor_tensor(
                        out=small[3][:, c:c + 1], in0=small[2][:, c:c + 1],
                        in1=strat_sig[:, qt * 24 + 3 * h + 1: qt * 24 + 3 * h + 2],
                        op=ALU.mult))
                    fn = lambda e, c=c, qt=qt, h=h: e.tensor_tensor(
                        out=small[3][:, 4 + c: 5 + c], in0=small[2][:, 4 + c: 5 + c],
                        in1=strat_sig[:, qt * 24 + 3 * h + 2: qt * 24 + 3 * h + 3],
                        op=ALU.mult)
                    t_sc = b.op_inc("dve", fn) if c == 3 else b.op("dve", fn)
                b.wait("dve", t_sc)
                tcb = None
                for c in range(4):
                    b.op("dve", lambda e, c=c, h=h: e.scalar_tensor_tensor(
                        out=att_acc[:, c * 1024 + h * 128: c * 1024 + (h + 1) * 128],
                        in0=pv_ev[:, c * 128:(c + 1) * 128], scalar=small[3][:, c:c + 1],
                        in1=att_acc[:, c * 1024 + h * 128: c * 1024 + (h + 1) * 128],
                        op0=ALU.mult, op1=ALU.add))
                for c in range(4):
                    fn = lambda e, c=c, h=h: e.scalar_tensor_tensor(
                        out=att_acc[:, c * 1024 + h * 128: c * 1024 + (h + 1) * 128],
                        in0=pv_ev[:, 512 + c * 128: 512 + (c + 1) * 128],
                        scalar=small[3][:, 4 + c: 5 + c],
                        in1=att_acc[:, c * 1024 + h * 128: c * 1024 + (h + 1) * 128],
                        op0=ALU.mult, op1=ALU.add)
                    tcb = b.op_inc("dve", fn) if c == 3 else b.op("dve", fn)
                return tcb

            gp = 0
            head_flags = {}
            for h in range(H):
                head_flags[h] = [True, True, True]
                for p in range(npair):
                    kt0 = 2 * p
                    pp = gp % 2
                    p3 = gp % P3N
                    Sps = ps_q if pp == 0 else ps_v
                    b.wait("pe", exp_done[pp])
                    if h == 0 and p <= 1:
                        b.wait("pe", comp_done[0])
                        b.wait("pe", mf_done[kmax])
                    tS = None
                    for ktip in range(2):
                        kt = kt0 + ktip
                        fadd = FADD and g * 512 - kt * 128 > 128
                        fnS = lambda e, h=h, kt=kt, g=g, Sps=Sps, ktip=ktip, fadd=fadd: \
                            e.matmul(
                                out=Sps[:, ktip * 512:(ktip + 1) * 512],
                                lhsT=k_fm[:, h * 2048 + kt * 128:
                                          h * 2048 + (kt + 1) * 128],
                                rhs=q_fm[:, h * 2048 + g * 512:
                                         h * 2048 + (g + 1) * 512],
                                start=True, stop=not fadd, skip_group_check=True)
                        if ktip == 1 and not fadd:
                            tS = b.op_inc("pe", fnS)
                        else:
                            b.op("pe", fnS)
                        if fadd:
                            fnM = lambda e, kt=kt, Sps=Sps, ktip=ktip: e.matmul(
                                out=Sps[:, ktip * 512:(ktip + 1) * 512],
                                lhsT=efull[:, kt * 128:(kt + 1) * 128],
                                rhs=selt250, start=False, stop=True,
                                skip_group_check=True)
                            tS = b.op_inc("pe", fnM) if ktip == 1 else b.op("pe", fnM)
                    if len(pending) >= max(1, PV_DEFER):
                        _, fcn = flush_pv()
                        if fcn is not None:
                            fine_combine = fcn
                    b.wait("act", tS)
                    b.wait("act", pv_done3[p3])
                    tE = b.op_inc("act", lambda e, p3=p3, Sps=Sps: e.activation(
                        out=e2x[p3], in_=Sps[:, 0:1024], func=AF.Exp, scale=SCALE))
                    exp_done[pp] = tE
                    tPF = None
                    halves = [(ktip, g * 512 - (kt0 + ktip) * 128) for ktip in range(2)]
                    if any(dp <= 128 for _, dp in halves) or not FADD:
                        b.wait("dve", tE)
                        b.wait("dve", pv_done3[p3])
                        for ktip, dp in halves:
                            if dp <= 128 or not FADD:
                                b.wait("dve", mf_done[kt0 + ktip])
                                tPF = b.op_inc("dve", lambda e, p3=p3, ktip=ktip,
                                               kt0=kt0: e.tensor_tensor(
                                    out=pf2x[p3][:, ktip * 512:(ktip + 1) * 512],
                                    in0=e2x[p3][:, ktip * 512:(ktip + 1) * 512],
                                    in1=mf01[:, (kt0 + ktip) * 512:
                                             (kt0 + ktip + 1) * 512], op=ALU.mult))
                            if -384 <= dp <= 128:
                                tPF = b.op_inc("dve", lambda e, p3=p3, ktip=ktip,
                                               dp=dp: e.tensor_tensor(
                                    out=psl2x[p3][:, ktip * 512:(ktip + 1) * 512],
                                    in0=e2x[p3][:, ktip * 512:(ktip + 1) * 512],
                                    in1=msm[:, dp + 384: dp + 384 + 512], op=ALU.mult))
                    pending.append((h, p, p3, tE, tPF))
                    gp += 1
                    if PV_DEFER == 0:
                        _, fcn = flush_pv()
                        if fcn is not None:
                            fine_combine = fcn
            while pending:
                _, fcn = flush_pv()
                if fcn is not None:
                    fine_combine = fcn

            if "ATT" in dbg_t:
                b.wait("sp", fine_combine)
                att_ev[g] = ld("sp", dbg_t["ATT"][g], att_acc, "ddbg")

            for c in range(4):
                qt = 4 * g + c
                b.wait("pe", fine_combine)
                b.wait("pe", cproj_done)
                tev = None
                for grp in range(2):
                    tgt = None
                    for cc in range(4):
                        ci = grp * 4 + cc
                        fn = lambda e, c=c, ci=ci, cc=cc: e.transpose(
                            out=ps_tr[:, cc * 128:(cc + 1) * 128],
                            in_=att_acc[:, c * 1024 + ci * 128: c * 1024 + (ci + 1) * 128],
                            identity=idb[:])
                        tgt = b.op_inc("pe", fn) if cc == 3 else b.op("pe", fn)
                    b.wait("act", tgt)
                    tev = b.op_inc("act", lambda e, grp=grp: e.copy(
                        out=attT[:, grp * 512:(grp + 1) * 512], in_=ps_tr[:]))
                    b.wait("pe", tev)
                b.wait("pe", tev)
                if g == 0 and c == 0:
                    b.wait("pe", t_cpw)
                b.wait("pe", comp_done[0])
                tcp = None
                for ci in range(8):
                    for hh in range(2):
                        fn = lambda e, ci=ci, hh=hh: e.matmul(
                            out=ps_v[:, hh * 512:(hh + 1) * 512],
                            lhsT=attT[:, ci * 128:(ci + 1) * 128],
                            rhs=cpw_sb[:, ci * 1024 + hh * 512: ci * 1024 + (hh + 1) * 512],
                            start=(ci == 0), stop=(ci == 7), skip_group_check=True)
                        tcp = b.op_inc("pe", fn) if (ci == 7 and hh == 1) else b.op("pe", fn)
                cproj_done = tcp
                b.wait("dve", tcp)
                if qt >= 1:
                    b.wait("dve", t_spill[qt - 1])
                tso = b.op_inc("dve", lambda e: e.tensor_copy(
                    out=spill_ev[:, 0:1024], in_=ps_v[:]))
                comp_done[0] = comp_done[1] = tso
                b.wait("sp", tso)
                t_spill[qt] = ld("sp", X1SPILL[qt * 128:(qt + 1) * 128, :],
                                 spill_ev[:, 0:1024], f"dsp{qt % 2}")

        b.wait("pool", cproj_done)
        t_wrt = [None] * 8
        for ci in range(8):
            t_wrt[ci] = ld("pool", wrt_sb[:, ci * 1024:(ci + 1) * 1024], WRT[ci], "dwrt")
        b.wait("pool", fine_combine)
        t_wvt = [None] * 8
        for ci in range(8):
            t_wvt[ci] = ld("pool", wvt6a[:, ci * 1024:(ci + 1) * 1024], WVT[ci], "dwvt")

        if "X1" in dbg_t:
            for qt in range(NT):
                b.wait("sp", t_spill[qt])
            ld("sp", dbg_t["X1"], X1SPILL, "ddbg")
        if phases <= 5:
            _finish(b, nc, en)
            return nc, dbg_t

        yv = y_fm.rearrange("p (c n) -> p c n", n=2049)
        b.wait("dve", fine_combine)
        t_yz = b.op_inc("dve", lambda e: e.memset(yv[:, :, 0:1], 0.0))
        t_x1 = [None] * NT
        t_x6 = [None] * NT
        b.wait("sp", t_spill[0])
        t_x1[0] = ld("sp", x1_ld[0][:, 0:1024], X1SPILL[0:128, :], "dx1a")
        b.wait("sp", t_spill[1])
        t_x1[1] = ld("sp", x1_ld[1][:, 0:1024], X1SPILL[128:256, :], "dx1b")
        t_x6[0] = ld("sp", x6_ld[0][:, 0:1024], X[0:128, :], "dx6a")
        t_x6[1] = ld("sp", x6_ld[1][:, 0:1024], X[128:256, :], "dx6b")

        yev_done = [None] * NT
        rk_done = [None] * NT
        rsig_done = [None] * NT
        karelu_done = [None] * NT
        kasq_done = [None] * NT
        katr_done = [None] * NT
        vout_done = [None] * NT
        xout_done = [None] * NT
        x1r_done = [None] * NT
        t_out = [None] * NT

        def emit_vout(qt):
            b.wait("pe", kasq_done[qt])
            if qt == 0:
                b.wait("pe", comp_done[0])
            tev = None
            for grp in range(2):
                tgt = None
                for cc in range(4):
                    ci = grp * 4 + cc
                    fn = lambda e, ci=ci, cc=cc: e.transpose(
                        out=ps_tr[:, cc * 128:(cc + 1) * 128],
                        in_=ka_tok[:, ci * 128:(ci + 1) * 128], identity=idb[:])
                    tgt = b.op_inc("pe", fn) if cc == 3 else b.op("pe", fn)
                b.wait("act", tgt)
                out_ap = kaT.rearrange("p (c t) -> p c t", t=2048)[
                    :, grp * 4:(grp + 1) * 4, qt * 128:(qt + 1) * 128]
                tev = b.op_inc("act", lambda e, o=out_ap: e.copy(
                    out=o, in_=ps_tr[:].rearrange("p (c t) -> p c t", t=128)))
                b.wait("pe", tev)
            katr_done[qt] = tev
            if qt >= 1:
                b.wait("pe", xout_done[qt - 1])
            tvo = None
            for ci in range(8):
                if qt == 0 and ci == 0:
                    for cj in range(8):
                        b.wait("pe", t_wvt[cj])
                for hh in range(2):
                    fn = lambda e, qt=qt, ci=ci, hh=hh: e.matmul(
                        out=ps_v[:, hh * 512:(hh + 1) * 512],
                        lhsT=kaT[:, ci * 2048 + qt * 128: ci * 2048 + (qt + 1) * 128],
                        rhs=wvt6a[:, ci * 1024 + hh * 512: ci * 1024 + (hh + 1) * 512],
                        start=(ci == 0), stop=(ci == 7), skip_group_check=True)
                    tvo = b.op_inc("pe", fn) if (ci == 7 and hh == 1) else b.op("pe", fn)
            vout_done[qt] = tvo
            b.wait("dve", tvo)
            b.wait("dve", rsig_done[qt])
            if qt >= 1:
                b.wait("dve", t_out[qt - 1])
            ob = qt % 3
            b.op("dve", lambda e: e.tensor_tensor(
                out=xout_sb[:, 0:512], in0=ps_v[:, 0:512], in1=r_sig[:, 0:512],
                op=ALU.mult))
            b.op("dve", lambda e: e.tensor_tensor(
                out=xout_sb[:, 512:1024], in0=ps_v[:, 512:1024], in1=r_sig[:, 512:1024],
                op=ALU.mult))
            b.op("dve", lambda e, ob=ob: e.tensor_tensor(
                out=xout_sb[:, 0:512], in0=xout_sb[:, 0:512],
                in1=x1_ld[ob][:, 0:512], op=ALU.add))
            txo = b.op_inc("dve", lambda e, ob=ob: e.tensor_tensor(
                out=xout_sb[:, 512:1024], in0=xout_sb[:, 512:1024],
                in1=x1_ld[ob][:, 512:1024], op=ALU.add))
            xout_done[qt] = txo
            b.wait("sp", txo)
            t_out[qt] = ld("sp", XOUT[qt * 128:(qt + 1) * 128, :], xout_sb[:, 0:1024],
                           f"do{qt % 2}")

        t_ln2_tab = [None] * NT
        ln2_pend = [None]

        def emit_ln2(qt):
            buf = qt % 3
            st = stat_scr[buf]
            yt = y_tok2[qt % 2]
            b.wait("dve", t_x1[qt])
            b.wait("dve", t_x6[qt])
            tx1r = b.op_inc("dve", lambda e, buf=buf: e.tensor_tensor(
                out=x1_ld[buf][:, 0:1024], in0=x1_ld[buf][:, 0:1024],
                in1=x6_ld[buf][:, 0:1024], op=ALU.add))
            x1r_done[qt] = tx1r
            b.wait("act", tx1r)
            if qt == 1:
                b.wait("act", t_spill[NT - 1])
            t_ssq = b.op_inc("act", lambda e, buf=buf, st=st, yt=yt: e.activation(
                out=yt, in_=x1_ld[buf][:, 0:1024], func=AF.Square, scale=32.0,
                accum_out=st[:, 1:2]))
            t_sum = b.op_inc("dve", lambda e, buf=buf, st=st: e.reduce_sum(
                out=st[:, 0:1], in_=x1_ld[buf][:, 0:1024], axis=AX.X))
            b.wait("dve", t_sum)
            b.wait("dve", t_ssq)
            t_u = b.op_inc("dve", lambda e, st=st: e.scalar_tensor_tensor(
                out=st[:, 4:5], in0=st[:, 0:1], scalar=st[:, 0:1], in1=st[:, 1:2],
                op0=ALU.mult, op1=ALU.subtract))
            b.op("dve", lambda e, st=st: e.tensor_scalar(
                out=st[:, 2:3], in0=st[:, 0:1], scalar1=OD, scalar2=None, op0=ALU.mult))
            b.wait("act", t_u)
            t_ln5 = b.op_inc("act", lambda e, st=st: e.activation(
                out=st[:, 5:6], in_=st[:, 4:5], func=AF.Ln, scale=-1.0,
                bias=epsT[:, 2:3]))
            b.wait("act", t_ln5)
            t_rstd = b.op_inc("act", lambda e, st=st: e.activation(
                out=st[:, 6:7], in_=st[:, 5:6], func=AF.Exp, scale=-0.5,
                bias=epsT[:, 3:4]))
            ln2_pend[0] = (qt, buf, st, yt, t_rstd)

        def emit_ln2_tail():
            qt, buf, st, yt, t_rstd = ln2_pend[0]
            ln2_pend[0] = None
            b.wait("dve", t_rstd)
            if qt == 0:
                b.wait("dve", t_yz)
            t_ln2_tab[qt] = b.op_inc("dve", lambda e, buf=buf, st=st, yt=yt:
                                     e.scalar_tensor_tensor(
                out=yt, in0=x1_ld[buf][:, 0:1024], scalar=st[:, 2:3],
                in1=st[:, 6:7].broadcast_to([128, 1024]),
                op0=ALU.subtract, op1=ALU.mult))

        emit_ln2(0)
        emit_ln2_tail()
        for qt in range(NT):
            buf = qt % 2
            b.wait("pe", t_ln2_tab[qt])
            tev = None
            for grp in range(2):
                tgt = None
                for cc in range(4):
                    ci = grp * 4 + cc
                    fn = lambda e, buf=buf, ci=ci, cc=cc: e.transpose(
                        out=ps_tr[:, cc * 128:(cc + 1) * 128],
                        in_=y_tok2[buf][:, ci * 128:(ci + 1) * 128], identity=idb[:])
                    tgt = b.op_inc("pe", fn) if cc == 3 else b.op("pe", fn)
                b.wait("act", tgt)
                out_ap = yv[:, grp * 4:(grp + 1) * 4, 1 + qt * 128: 1 + (qt + 1) * 128]
                tev = b.op_inc("act", lambda e, o=out_ap: e.copy(
                    out=o, in_=ps_tr[:].rearrange("p (c t) -> p c t", t=128)))
                b.wait("pe", tev)
            yev_done[qt] = tev
            if qt + 1 < NT:
                emit_ln2(qt + 1)

            b.wait("dve", tev)
            if qt >= 2:
                b.wait("dve", rk_done[qt - 2])
            tmx = None
            ypre_v = yv[:, :, qt * 128:(qt + 1) * 128]
            ycur_v = yv[:, :, 1 + qt * 128: 1 + (qt + 1) * 128]
            for s, mu, omu, dst in ((0, mur_sb, omur_sb, xr_sc2[buf]),
                                    (1, muk_sb, omuk_sb, xk_sc2[buf])):
                dv = dst.rearrange("p (c n) -> p c n", n=128)
                mub = mu[:].unsqueeze(2).broadcast_to([128, 8, 128])
                omub = omu[:].unsqueeze(2).broadcast_to([128, 8, 128])
                b.op("dve", lambda e, dv=dv, omub=omub, ypre_v=ypre_v: e.tensor_tensor(
                    out=dv, in0=ypre_v, in1=omub, op=ALU.mult))
                b.op("dve", lambda e, mub=mub, ycur_v=ycur_v: e.tensor_tensor(
                    out=mix_tmp.rearrange("p (c n) -> p c n", n=128), in0=ycur_v,
                    in1=mub, op=ALU.mult))
                fn = lambda e, dst=dst: e.tensor_tensor(
                    out=dst, in0=dst, in1=mix_tmp, op=ALU.add)
                tmx = b.op_inc("dve", fn) if s == 1 else b.op("dve", fn)

            if ln2_pend[0] is not None:
                emit_ln2_tail()

            b.wait("pe", tmx)
            if qt == 0:
                for cj in range(8):
                    b.wait("pe", t_wrt[cj])
                b.wait("pe", t_wkt)
            if qt >= 1:
                b.wait("pe", rsig_done[qt - 1])
                b.wait("pe", karelu_done[qt - 1])
            trk = None
            for ci in range(8):
                for hh in range(2):
                    b.op("pe", lambda e, buf=buf, ci=ci, hh=hh: e.matmul(
                        out=ps_q[:, hh * 512:(hh + 1) * 512],
                        lhsT=xr_sc2[buf][:, ci * 128:(ci + 1) * 128],
                        rhs=wrt_sb[:, ci * 1024 + hh * 512: ci * 1024 + (hh + 1) * 512],
                        start=(ci == 0), stop=(ci == 7), skip_group_check=True))
                    fn = lambda e, buf=buf, ci=ci, hh=hh: e.matmul(
                        out=ps_k[:, hh * 512:(hh + 1) * 512],
                        lhsT=xk_sc2[buf][:, ci * 128:(ci + 1) * 128],
                        rhs=wkt_sb[:, ci * 1024 + hh * 512: ci * 1024 + (hh + 1) * 512],
                        start=(ci == 0), stop=(ci == 7), skip_group_check=True)
                    trk = b.op_inc("pe", fn) if (ci == 7 and hh == 1) else b.op("pe", fn)
            rk_done[qt] = trk
            if qt >= 1:
                emit_vout(qt - 1)
            if qt + 2 < NT:
                nb = (qt + 2) % 3
                b.wait("sp", t_spill[qt + 2])
                b.wait("sp", xout_done[qt - 1])
                b.wait("sp", x1r_done[qt - 1])
                t_x1[qt + 2] = ld("sp", x1_ld[nb][:, 0:1024],
                                  X1SPILL[(qt + 2) * 128:(qt + 3) * 128, :],
                                  f"dx1{'abc'[nb]}")
                t_x6[qt + 2] = ld("sp", x6_ld[nb][:, 0:1024],
                                  X[(qt + 2) * 128:(qt + 3) * 128, :],
                                  f"dx6{'abc'[nb]}")

            b.wait("act", trk)
            if qt >= 1:
                b.wait("act", xout_done[qt - 1])
            trs = b.op_inc("act", lambda e: e.activation(
                out=r_sig[:], in_=ps_q[:], func=AF.Sigmoid))
            rsig_done[qt] = trs
            b.wait("dve", trk)
            if qt >= 1:
                b.wait("dve", kasq_done[qt - 1])
            trl = b.op_inc("dve", lambda e: e.tensor_scalar_max(
                out=k2relu[:, 0:1024], in0=ps_k[:], scalar1=0.0))
            karelu_done[qt] = trl
            b.wait("act", trl)
            if qt >= 1:
                b.wait("act", katr_done[qt - 1])
            tks = b.op_inc("act", lambda e: e.activation(
                out=ka_tok[:], in_=k2relu[:, 0:1024], func=AF.Square))
            kasq_done[qt] = tks

        emit_vout(NT - 1)
        if "YFM" in dbg_t:
            b.wait("sp", yev_done[NT - 1])
            ld("sp", dbg_t["YFM"], y_fm, "ddbg")
        if "KAT" in dbg_t:
            b.wait("sp", katr_done[NT - 1])
            ld("sp", dbg_t["KAT"], kaT, "ddbg")
        if "RSIG15" in dbg_t:
            b.wait("sp", rsig_done[NT - 1])
            ld("sp", dbg_t["RSIG15"], r_sig[:], "ddbg")
        if "X1LD15" in dbg_t:
            b.wait("sp", x1r_done[NT - 1])
            ld("sp", dbg_t["X1LD15"], x1_ld[(NT - 1) % 2][:, 0:1024], "ddbg")
        _finish(b, nc, en)
    return nc, dbg_t


def _finish(b, nc, en):
    sems = {}
    for name in sorted(b.sem_names):
        sems[name] = en(nc.semaphore(name))
    for name, val in sorted(b.dma_cnt.items()):
        b.q["sp"].append(("wait", name, val))
    block = en(nc.Block())

    @block.tensor
    def _(tensor):
        b.replay("pe", tensor, sems)

    @block.scalar
    def _(scalar):
        b.replay("act", scalar, sems)

    @block.vector
    def _(vector):
        b.replay("dve", vector, sems)

    @block.gpsimd
    def _(gpsimd):
        b.replay("pool", gpsimd, sems)

    @block.sync
    def _(sync):
        b.replay("sp", sync, sems)


def kernel(**inputs):
    in_maps = host_prepare(inputs)
    nc, _ = build_program()
    res = run_bass_kernel_spmd(nc, in_maps, core_ids=list(range(8)))
    return np.stack([res.results[c]["XOUT"] for c in range(8)], 0)
